# version 09 (final)
# speedup vs baseline: 1.1986x; 1.0360x over previous
from contextlib import ExitStack

import numpy as np
import concourse.bass as bass
import concourse.mybir as mybir

F32 = mybir.dt.float32
BF16 = mybir.dt.bfloat16
U8 = mybir.dt.uint8
AF = mybir.ActivationFunctionType
ALU = mybir.AluOpType
AX = mybir.AxisListType

ENGS = ("sync", "tensor", "vector", "scalar", "gpsimd")


class Rec:
    def __init__(self, eng, attr):
        self.eng = eng
        self.attr = attr

    def __call__(self, *a, **k):
        pb = self.eng.pb
        inc = k.pop("inc", None)
        wait = k.pop("wait", None)
        waits = []
        if wait:
            if isinstance(wait, tuple):
                wait = [wait]
            for (s, v) in wait:
                waits.append((s, v))
        incs = []
        if inc:
            if isinstance(inc, tuple):
                inc = [inc]
            for item in inc:
                if isinstance(item, str):
                    item = (item, 1)
                s, v = item
                pb.sem_count[s] += v
                incs.append((s, v))
        self.eng.cmds.append((self.attr, a, k, waits, incs))
        if self.eng.name == "gpsimd" and self.attr == "dma_start" and incs:
            for (s_, v_) in incs:
                self.eng.wait(s_, pb.sem_count[s_])


class Eng:
    def __init__(self, pb, name):
        self.pb = pb
        self.name = name
        self.cmds = []

    def __getattr__(self, attr):
        if attr.startswith("_"):
            raise AttributeError(attr)
        return Rec(self, attr)

    def wait(self, s, v=None):
        if v is None:
            v = self.pb.sem_count[s]
        self.cmds.append((None, None, None, [(s, v)], []))

    def raw(self, fn, incs=()):
        for (s, v) in incs:
            self.pb.sem_count[s] += v
        self.cmds.append(("__RAW__", fn, None, [], []))


class PB:
    def __init__(self, name="mk"):
        import os as _os
        self.nc = bass.Bass("TRN2", target_bir_lowering=False,
                            detect_race_conditions=bool(_os.environ.get("RACEDET")))
        self.ctx = ExitStack()
        self.sems = {}
        self.sem_count = {}
        self.e = {n: Eng(self, n) for n in ENGS}
        self.Y = self.e["sync"]
        self.T = self.e["tensor"]
        self.V = self.e["vector"]
        self.S = self.e["scalar"]
        self.G = self.e["gpsimd"]
        self.inputs = {}
        self.outputs = {}
        self.tensors = {}
        self._consts = {}

    def sem(self, name):
        if name not in self.sems:
            self.sems[name] = self.ctx.enter_context(self.nc.semaphore(name))
            self.sem_count[name] = 0
        return self.sems[name]

    def inp(self, name, shape, dtype=F32):
        ap = self.nc.dram_tensor(name, list(shape), dtype, kind="ExternalInput").ap()
        self.inputs[name] = ap
        return ap

    def out(self, name, shape, dtype=F32):
        ap = self.nc.dram_tensor(name, list(shape), dtype, kind="ExternalOutput").ap()
        self.outputs[name] = ap
        return ap

    def dram(self, name, shape, dtype=F32):
        ap = self.nc.dram_tensor(name, list(shape), dtype, kind="Internal").ap()
        self.tensors[name] = ap
        return ap

    def sbuf(self, name, shape, dtype=F32):
        t = self.ctx.enter_context(self.nc.sbuf_tensor(name, list(shape), dtype))
        self.tensors[name] = t
        return t

    def psum(self, name, shape, dtype=F32):
        t = self.ctx.enter_context(self.nc.psum_tensor(name, list(shape), dtype))
        self.tensors[name] = t
        return t

    def cf(self, v, dtype=F32):
        key = (float(v), dtype)
        if key not in self._consts:
            assert dtype == F32
            t = self._cfpool[len(self._consts)]
            self.V.memset(t[:], float(v), inc=("v", 1))
            self._consts[key] = t
        return self._consts[key]

    def n(self, s):
        return self.sem_count[s]

    def barrier(self, consumers=ENGS, producers=None):
        if producers is None:
            producers = list(self.sems.keys())
        for c in consumers:
            for s in producers:
                v = self.sem_count[s]
                if v > 0:
                    self.e[c].wait(s, v)

    def finish(self):
        nc = self.nc
        for s in self.sems.values():
            pass
        with nc.Block() as block:
            for name in ENGS:
                eng_rec = self.e[name]
                if not eng_rec.cmds:
                    continue
                cmds = eng_rec.cmds

                def make_run(cmds=cmds):
                    def run(eng):
                        for (attr, a, k, waits, incs) in cmds:
                            for (s, v) in waits:
                                eng.wait_ge(self.sems[s], v)
                            if attr is None:
                                continue
                            if attr == "__RAW__":
                                a(eng, self)
                                continue
                            instr = getattr(eng, attr)(*a, **k)
                            for (s, v) in incs:
                                instr.then_inc(self.sems[s], v)
                    return run

                getattr(block, name)(make_run())
        self.ctx.close()
        return self.nc



def setup_common(pb):
    for s in ("ld", "lw", "st", "t", "v", "s", "g", "cc"):
        pb.sem(s)
    pb.ps = [pb.psum(f"ps{i}", [128, 512], F32) for i in range(4)]
    pb.ps_s1 = pb.psum("ps_s1", [128, 512], F32)
    pb.ps_s2 = pb.psum("ps_s2", [128, 512], F32)
    pb.pst = pb.psum("pst", [128, 1024], BF16)
    pb.pst2 = pb.psum("pst2", [128, 1024], BF16)
    pb.warena = pb.sbuf("warena", [128, 2 * 4096], BF16)
    pb._wfree = [0, 0]
    pb.ones_r = pb.sbuf("ones_r", [1, 128], F32)
    pb.ones_c = pb.sbuf("ones_c", [128, 1], F32)
    pb.ones_cb = pb.sbuf("ones_cb", [128, 1], BF16)
    pb.V.memset(pb.ones_r[:], 1.0)
    pb.V.memset(pb.ones_c[:], 1.0)
    pb.V.memset(pb.ones_cb[:], 1.0, inc=("v", 1))
    pb.strow = pb.sbuf("strow", [1, 2048], F32)
    pb.sqar = pb.sbuf("sqar", [128, 8 * 512], F32)
    pb.tmpa = pb.sbuf("tmpa", [128, 512], F32)
    pb.tmpb = pb.sbuf("tmpb", [128, 512], F32)
    pb.spa = pb.sbuf("spa", [128, 512], F32)
    pb.spb = pb.sbuf("spb", [128, 512], F32)
    pb._cfpool = [pb.sbuf(f"c_{i}", [128, 1], F32) for i in range(8)]
    pb.sbuf("negB", [128, 1], F32)


def setup_arena(pb, kbytes=100):
    pb.arena = pb.sbuf("bigarena", [128, kbytes * 1024], U8)


def setup_rank(pb):
    pb.pidt = pb.sbuf("pidt", [1, 1], mybir.dt.uint32)
    pb.G.dma_start(out=pb.pidt[:], in_=pb.nc.partition_id_tensor.ap()[:, :],
                   inc=("lw", 16))


def rank_switch(pb, eng_name, mod, fns, incs=()):
    def f(eng, pbx):
        with eng.register() as r:
            eng.reg_load(r, pbx.pidt[0:1, 0:1])
            with eng.register() as r2:
                eng.reg_alu(r2, r, mod, ALU.mod)
                for i in range(mod):
                    if i == 0:
                        ctx = eng.If_eq(r2, 0)
                    else:
                        ctx = eng.Else() if mod == 2 else None
                    with ctx:
                        fns[i](eng, pbx)
    pb.e[eng_name].raw(f, incs=incs)


def view(pb, off, cols, dtype=BF16):
    nb = cols * (2 if dtype in (BF16,) else 4)
    return pb.arena[:, off:off + nb].bitcast(dtype)


def wslot(pb, slot, kc):
    return pb.warena[:, slot * 4096 + kc * 128:slot * 4096 + (kc + 1) * 128]


def load_w(pb, W, K, og, slot, q="gpsimd", ocols=128, owid=None):
    if owid is None:
        owid = ocols
    pb.e[q].wait("t", pb._wfree[slot])
    lsem = "lw" if q == "gpsimd" else "ld"
    if isinstance(W, tuple):
        _, ap, kt = W
        pb.e[q].dma_start(out=pb.warena[:, slot * 4096: slot * 4096 + kt * 128],
                          in_=ap[og * 128:(og + 1) * 128, 0:kt * 128],
                          inc=(lsem, 16))
    else:
        kt = K // 128
        for kc in range(kt):
            pb.e[q].dma_start(
                out=pb.warena[:, slot * 4096 + kc * 128: slot * 4096 + kc * 128 + owid],
                in_=W[kc * 128:(kc + 1) * 128, og * ocols: og * ocols + owid],
                inc=(lsem, 16))



def layernorm(pb, x_get, Tl, out_cb, eps=1e-5, kt=8, inv_n=None, bf_in=False,
              w=512, rms=False):
    ntt = Tl // w
    if inv_n is None:
        inv_n = 1.0 / (kt * 128.0)
    assert w <= 512
    sx = pb.strow[:, 0:w]
    sxx = pb.strow[:, 512:512 + w]
    A = pb.strow[:, 1024:1024 + w]
    B = pb.strow[:, 1536:1536 + w]
    onesx = pb.ones_cb if bf_in else pb.ones_c
    for tt in range(ntt):
        for kc in range(kt):
            pb.S.activation(out=pb.sqar[:, kc * 512:kc * 512 + w], in_=x_get(kc, tt),
                            func=AF.Square, inc=("s", 1),
                            wait=[("t", pb.n("t")), ("v", pb.n("v"))] if kc == 0 else None)
        if not rms:
            for kc in range(kt):
                pb.T.matmul(pb.ps_s1[:1, 0:w], onesx[:, :], x_get(kc, tt),
                            start=(kc == 0), stop=(kc == kt - 1),
                            wait=[("v", pb.n("v"))] if kc == 0 else None)
        for kc in range(kt):
            pb.T.matmul(pb.ps_s2[:1, 0:w], pb.ones_c[:, :],
                        pb.sqar[:, kc * 512:kc * 512 + w],
                        start=(kc == 0), stop=(kc == kt - 1),
                        wait=[("s", pb.n("s"))] if kc == 0 else None,
                        inc=("t", 1) if kc == kt - 1 else None)
        if rms:
            pb.V.tensor_scalar_mul(sxx[:], pb.ps_s2[:1, 0:w], inv_n,
                                   wait=[("t", pb.n("t"))], inc=("v", 1))
            pb.S.activation(out=sxx[:], in_=sxx[:], func=AF.Ln, bias=pb.cf(eps)[:1, :],
                            wait=[("v", pb.n("v"))], inc=("s", 1))
            pb.S.activation(out=A[:], in_=sxx[:], func=AF.Exp, scale=-0.5,
                            wait=[("s", pb.n("s"))], inc=("s", 1))
            pb.T.matmul(pb.ps_s1[:, 0:w], pb.ones_r[:, :], A[:],
                        wait=[("s", pb.n("s"))], start=True, stop=True,
                        inc=("t", 1))
            out_cb(tt, pb.ps_s1[:, 0:w], None)
            continue
        pb.V.tensor_scalar_mul(sx[:], pb.ps_s1[:1, 0:w], inv_n,
                               wait=[("t", pb.n("t"))], inc=("v", 1))
        pb.V.tensor_scalar_mul(sxx[:], pb.ps_s2[:1, 0:w], inv_n, inc=("v", 1))
        pb.S.square(out=A[:], in_=sx[:], wait=[("v", pb.n("v"))], inc=("s", 1))
        pb.V.tensor_sub(sxx[:], sxx[:], A[:], wait=[("s", pb.n("s"))], inc=("v", 1))
        pb.S.activation(out=sxx[:], in_=sxx[:], func=AF.Ln, bias=pb.cf(eps)[:1, :],
                        wait=[("v", pb.n("v"))], inc=("s", 1))
        pb.S.activation(out=A[:], in_=sxx[:], func=AF.Exp, scale=-0.5,
                        wait=[("s", pb.n("s"))], inc=("s", 1))
        pb.V.scalar_tensor_tensor(B[:], sx[:], -1.0, A[:], ALU.mult, ALU.mult,
                                  wait=[("s", pb.n("s"))], inc=("v", 1))
        pb.T.matmul(pb.ps_s1[:, 0:w], pb.ones_r[:, :], A[:],
                    wait=[("v", pb.n("v"))], start=True, stop=True)
        pb.T.matmul(pb.ps_s2[:, 0:w], pb.ones_r[:, :], B[:],
                    start=True, stop=True, inc=("t", 1))
        out_cb(tt, pb.ps_s1[:, 0:w], pb.ps_s2[:, 0:w])


def mm(pb, W, K, O, rhs_get, Tl, consume, ldq="gpsimd", free_sem="v",
       acc_first=True, acc_last=True, korder=True):
    if isinstance(W, tuple):
        kt = W[2]
    else:
        kt = K // 128
    assert kt <= 32
    nog = (O + 127) // 128
    ntt = (Tl + 511) // 512
    lsem = "lw" if ldq == "gpsimd" else "ld"
    if not hasattr(pb, "_bankfree"):
        pb._bankfree = [("v", 0)] * 4
    simple = not (acc_first and acc_last and korder)
    for og in range(nog):
        slot = og % 2
        ow = min(128, O - og * 128)
        load_w(pb, W, K, og, slot, q=ldq, ocols=128, owid=ow)
        ld_target = pb.n(lsem)
        if simple:
            for tt in range(ntt):
                bank = (og * ntt + tt) % 4
                tw = min(512, Tl - tt * 512)
                bfs, bfc = pb._bankfree[bank]
                for kc in range(kt):
                    w_ = None
                    if kc == 0:
                        w_ = [(lsem, ld_target)]
                        if acc_first:
                            w_.append((bfs, bfc))
                    pb.T.matmul(pb.ps[bank][0:ow, 0:tw],
                                wslot(pb, slot, kc)[:, 0:ow], rhs_get(kc, tt),
                                start=(kc == 0 and acc_first),
                                stop=(kc == kt - 1 and acc_last),
                                wait=w_, inc=("t", 1) if kc == kt - 1 else None)
                if acc_last:
                    consume(og, tt, pb.ps[bank], pb.n("t"))
                    pb._bankfree[bank] = (free_sem, pb.n(free_sem))
            pb._wfree[slot] = pb.n("t")
            continue
        for t0 in range(0, ntt, 4):
            tts = list(range(t0, min(t0 + 4, ntt)))
            for kc in range(kt):
                for tt in tts:
                    bank = (og * ntt + tt) % 4
                    tw = min(512, Tl - tt * 512)
                    w_ = None
                    if kc == 0:
                        w_ = [pb._bankfree[bank]]
                        if tt == tts[0]:
                            w_.append((lsem, ld_target))
                    pb.T.matmul(pb.ps[bank][0:ow, 0:tw],
                                wslot(pb, slot, kc)[:, 0:ow], rhs_get(kc, tt),
                                start=(kc == 0), stop=(kc == kt - 1),
                                wait=w_, inc=("t", 1) if kc == kt - 1 else None)
            for tt in tts:
                bank = (og * ntt + tt) % 4
                consume(og, tt, pb.ps[bank], pb.n("t"))
                pb._bankfree[bank] = (free_sem, pb.n(free_sem))
        pb._wfree[slot] = pb.n("t")


import concourse.mybir as _mybir
FP16 = _mybir.dt.float16

TL = 2048
NT = TL // 512


def xg(buf, Tl=TL, t0=0, w=512):
    def g(kc, tt):
        return buf[:, kc * Tl + t0 + tt * w: kc * Tl + t0 + tt * w + w]
    return g


def emit_ln_to(pb, src_buf, dst_buf, dst2=None, eps=1e-5, Tl=TL, kt=8,
               add_to=None, gain=None, bf_in=False, w=512, t0=0, span=None,
               dTl=None, dt0=None, rms=False, eps2=None):
    if span is None:
        span = Tl - t0
    if dTl is None:
        dTl = Tl
    if dt0 is None:
        dt0 = t0
    tmp = pb.tmpa
    tmp2 = pb.tmpb
    g = xg(src_buf, Tl, t0=t0, w=w)
    gd = lambda kc, tt: slice(kc * dTl + dt0 + tt * w, kc * dTl + dt0 + tt * w + w)

    def cb(tt, A_ps, B_ps):
        for k0 in range(0, kt, 2):
            sls = [gd(kc, tt) for kc in (k0, k0 + 1)]
            tmps = [tmp[:, 0:w], tmp2[:, 0:w]]
            w_ = [("t", pb.n("t"))] if k0 == 0 else None
            if B_ps is None:
                if gain is None:
                    pb.V.tensor_mul(dst_buf[:, sls[0]], g(k0, tt), A_ps, wait=w_)
                    pb.V.tensor_mul(dst_buf[:, sls[1]], g(k0 + 1, tt), A_ps,
                                    inc=("v", 1))
                else:
                    gt = gain[0]
                    pb.V.tensor_mul(tmps[0], g(k0, tt), A_ps, wait=w_)
                    pb.V.tensor_mul(tmps[1], g(k0 + 1, tt), A_ps)
                    pb.V.tensor_scalar_mul(dst_buf[:, sls[0]], tmps[0],
                                           gt[:, k0:k0 + 1])
                    pb.V.tensor_scalar_mul(dst_buf[:, sls[1]], tmps[1],
                                           gt[:, k0 + 1:k0 + 2], inc=("v", 1))
                continue
            if add_to is not None:
                pb.V.tensor_mul(tmps[0], g(k0, tt), A_ps, wait=w_)
                pb.V.tensor_mul(tmps[1], g(k0 + 1, tt), A_ps)
                pb.V.tensor_add(tmps[0], tmps[0], B_ps)
                pb.V.tensor_add(tmps[1], tmps[1], B_ps)
                pb.V.tensor_add(add_to[:, sls[0]], add_to[:, sls[0]], tmps[0])
                pb.V.tensor_add(add_to[:, sls[1]], add_to[:, sls[1]], tmps[1],
                                inc=("v", 1))
            elif gain is None:
                pb.V.tensor_mul(tmps[0], g(k0, tt), A_ps, wait=w_)
                pb.V.tensor_mul(tmps[1], g(k0 + 1, tt), A_ps)
                pb.V.tensor_add(dst_buf[:, sls[0]], tmps[0], B_ps)
                pb.V.tensor_add(dst_buf[:, sls[1]], tmps[1], B_ps, inc=("v", 1))
                if dst2 is not None:
                    pb.V.tensor_copy(out=dst2[:, sls[0]], in_=dst_buf[:, sls[0]])
                    pb.V.tensor_copy(out=dst2[:, sls[1]], in_=dst_buf[:, sls[1]],
                                     inc=("v", 1))
            else:
                gt, bt = gain
                pb.V.tensor_mul(tmps[0], g(k0, tt), A_ps, wait=w_)
                pb.V.tensor_mul(tmps[1], g(k0 + 1, tt), A_ps)
                pb.V.tensor_add(tmps[0], tmps[0], B_ps)
                pb.V.tensor_add(tmps[1], tmps[1], B_ps)
                pb.V.tensor_scalar(dst_buf[:, sls[0]], tmps[0],
                                   gt[:, k0:k0 + 1], bt[:, k0:k0 + 1],
                                   ALU.mult, ALU.add)
                pb.V.tensor_scalar(dst_buf[:, sls[1]], tmps[1],
                                   gt[:, k0 + 1:k0 + 2], bt[:, k0 + 1:k0 + 2],
                                   ALU.mult, ALU.add, inc=("v", 1))
    layernorm(pb, g, span, cb, eps=(eps2 if rms else eps), kt=kt,
              bf_in=bf_in, w=w, rms=rms)


def emit_l3ffn_weights_prep(pb, th, tn, ph, pn, cosw, sinw):
    a = pb.sqar[:, 0:512]
    b = pb.sqar[:, 512:1024]
    ob = pb.warena.bitcast(BF16)
    for (wa, wb, dst, phase) in ((th, tn, cosw, np.pi / 2), (ph, pn, sinw, 0.0)):
        for r in range(8):
            for c in range(2):
                sl_in = (slice(r * 128, r * 128 + 128), slice(c * 512, c * 512 + 512))
                ld0 = pb.n("ld")
                pb.Y.dma_start(out=a[:], in_=wa[sl_in],
                               wait=[("v", pb.n("v")), ("s", pb.n("s"))],
                               inc=("ld", 16))
                pb.Y.dma_start(out=b[:], in_=wb[sl_in], inc=("ld", 16))
                pb.V.tensor_add(a[:], a[:], b[:], wait=[("ld", pb.n("ld")),
                                                        ("st", pb.n("st"))],
                                inc=("v", 1))
                pb.S.activation(out=pb.warena[:, 0:512], in_=a[:], func=AF.Sin,
                                bias=pb.cf(phase)[:, :] if phase else 0.0,
                                wait=[("v", pb.n("v")), ("st", pb.n("st"))],
                                inc=("s", 1))
                pb.Y.dma_start(out=dst[sl_in], in_=pb.warena[:, 0:512],
                               wait=[("s", pb.n("s"))], inc=("st", 16))
    pb.barrier()


def emit_l3ffn(pb, xn, acc, cosw, sinw, Tl=TL):
    rhs = xg(xn, Tl)
    kt = 8
    s1 = pb.sqar[:, 0:512]
    s2 = pb.sqar[:, 512:1024]
    u = pb.sqar[:, 1024:1536]
    for og in range(8):
        pb.G.wait("t", pb._wfree[0])
        for kc in range(kt):
            pb.G.dma_start(out=wslot(pb, 0, kc), in_=cosw[kc * 128:(kc + 1) * 128,
                                                         og * 128:(og + 1) * 128],
                           inc=("lw", 16))
            pb.G.dma_start(out=wslot(pb, 1, kc), in_=sinw[kc * 128:(kc + 1) * 128,
                                                          og * 128:(og + 1) * 128],
                           inc=("lw", 16))
        ldt = pb.n("lw")
        for tt in range(Tl // 512):
            sfree = pb.n("s")
            for kc in range(kt):
                w_ = [("lw", ldt), ("s", getattr(pb, "_l3f_free", 0))] if kc == 0 else None
                pb.T.matmul(pb.ps[0][:, 0:512], wslot(pb, 0, kc), rhs(kc, tt),
                            start=(kc == 0), stop=(kc == kt - 1), wait=w_)
                pb.T.matmul(pb.ps[1][:, 0:512], wslot(pb, 1, kc), rhs(kc, tt),
                            start=(kc == 0), stop=(kc == kt - 1),
                            inc=("t", 1) if kc == kt - 1 else None)
            pb.S.activation(out=s1[:], in_=pb.ps[0][:, 0:512], func=AF.Square,
                            wait=[("t", pb.n("t")), ("v", pb.n("v"))])
            pb.S.activation(out=s2[:], in_=pb.ps[1][:, 0:512], func=AF.Square,
                            inc=("s", 2))
            pb._l3f_free = pb.n("s")
            pb.V.tensor_add(u[:], s1[:], s2[:], wait=[("s", pb.n("s"))],
                            inc=("v", 1))
            pb.S.activation(out=u[:], in_=u[:], func=AF.Ln,
                            wait=[("v", pb.n("v"))])
            pb.S.activation(out=acc[:, og * Tl + tt * 512: og * Tl + tt * 512 + 512],
                            in_=u[:], func=AF.Exp, scale=0.5, inc=("s", 1))
        pb._wfree[0] = pb.n("t")
    pb.barrier()


def emit_prep_transposed(pb, W, O, K, out_tiled, idb):
    noc = O // 128
    kt = K // 128
    stage_in = pb.warena[:, 0:K]
    stage_out = pb.warena[:, 4096:4096 + K]
    assert K <= 4096
    for oc in range(noc):
        pb.G.dma_start(out=stage_in[:, :], in_=W[oc * 128:(oc + 1) * 128, :],
                       wait=[("t", pb.n("t")), ("st", pb.n("st"))],
                       inc=("lw", 16))
        lwt = pb.n("lw")
        v_base = pb.n("v")
        for kc in range(kt):
            bankt = pb.pst if kc % 2 == 0 else pb.pst2
            sl = (kc // 2) % 4
            if kc < 2:
                w_ = [("lw", lwt), ("v", v_base)]
            else:
                w_ = [("v", v_base + kc - 1)]
            pb.T.transpose(bankt[:, sl * 128:(sl + 1) * 128],
                           stage_in[:, kc * 128:(kc + 1) * 128], idb[:, :],
                           wait=w_, inc=("t", 1))
            pb.V.tensor_copy(out=stage_out[:, kc * 128:(kc + 1) * 128],
                             in_=bankt[:, sl * 128:(sl + 1) * 128],
                             wait=[("t", pb.n("t"))], inc=("v", 1))
        pb.Y.dma_start(out=out_tiled[oc * 128:(oc + 1) * 128, 0:kt * 128],
                       in_=stage_out[:, 0:kt * 128], wait=[("v", pb.n("v"))],
                       inc=("st", 16))
    pb.barrier()


def emit_l0ffn(pb, xres, acc, w1T, w2T, h1T, h2T, w3T, s1, s2, sh1, sh2, s3,
               hbufA, hbufB, xnw, Tl=TL):
    W = 512
    for tt in range(Tl // W):
        emit_ln_to(pb, xres, xnw, w=W, t0=tt * W, span=W, dTl=W, dt0=0)
        pb.barrier(consumers=("tensor",), producers=["v"])
        rhs_x = lambda kc, _tt: xnw[:, kc * W:(kc + 1) * W]
        rhs_A = lambda kc, _tt: hbufA[:, kc * W:(kc + 1) * W]
        rhs_B = lambda kc, _tt: hbufB[:, kc * W:(kc + 1) * W]

        def mk_act(dst, sc, func):
            def c(og, _t, ps, t_need):
                pb.S.activation(out=dst[:, og * W:(og + 1) * W],
                                in_=ps[:, 0:W], func=func,
                                scale=sc[:, og:og + 1],
                                wait=[("t", t_need)], inc=("s", 1))
            return c

        mm(pb, ("til", w1T, 8), 1024, 4096, rhs_x, W,
           mk_act(hbufA, s1, AF.Silu), free_sem="s")
        pb.T.wait("s")

        def c_mul(og, _t, ps, t_need):
            sl = slice(og * W, (og + 1) * W)
            pb.V.scalar_tensor_tensor(hbufA[:, sl], ps[:, 0:W],
                                      s2[:, og:og + 1], hbufA[:, sl],
                                      ALU.mult, ALU.mult,
                                      wait=[("t", t_need), ("s", pb.n("s"))],
                                      inc=("v", 1))
        mm(pb, ("til", w2T, 8), 1024, 4096, rhs_x, W, c_mul, free_sem="v")
        pb.T.wait("v")
        mm(pb, ("til", h1T, 32), 4096, 4096, rhs_A, W,
           mk_act(hbufB, sh1, AF.Gelu), free_sem="s")
        pb.T.wait("s")
        mm(pb, ("til", h2T, 32), 4096, 4096, rhs_B, W,
           mk_act(hbufA, sh2, AF.Gelu), free_sem="s")
        pb.T.wait("s")

        def c_out(og, _t, ps, t_need):
            pb.S.activation(out=acc[:, og * W:(og + 1) * W],
                            in_=ps[:, 0:W], func=AF.Copy,
                            scale=s3[:, og:og + 1],
                            wait=[("t", t_need)], inc=("s", 1))
        mm(pb, ("til", w3T, 32), 4096, 1024, rhs_A, W, c_out, free_sem="s")
        pb.T.wait("s")
        pb.barrier(consumers=("tensor", "vector", "scalar"), producers=["s"])
        emit_ln_to(pb, acc, None, add_to=xres, bf_in=True, w=W, Tl=W, t0=0,
                   span=W, dTl=Tl, dt0=tt * W)
    pb.barrier()


def emit_l2ffn(pb, xres, sgT, suT, sdT, routerT, rb, gwT, uwT, dwT, sel8, idm,
               xnw, gbuf, ubuf, acc, Tl=TL):
    W = 512
    for tt in range(Tl // W):
        emit_ln_to(pb, xres, xnw, w=W, t0=tt * W, span=W, dTl=W, dt0=0)
        pb.barrier(consumers=("tensor",), producers=["v"])
        rhs_x = lambda kc, _tt: xnw[:, kc * W:(kc + 1) * W]
        rhs_g = lambda kc, _tt: gbuf[:, kc * W:(kc + 1) * W]
        pb.V.memset(acc[:, :], 0.0, inc=("v", 1))

        def c_silu(og, _t, ps, t_need):
            pb.S.activation(out=gbuf[:, og * W:(og + 1) * W], in_=ps[:, 0:W],
                            func=AF.Silu, wait=[("t", t_need)], inc=("s", 1))

        def c_mulu(og, _t, ps, t_need):
            sl = slice(og * W, (og + 1) * W)
            pb.V.scalar_tensor_tensor(gbuf[:, sl], ps[:, 0:W], 0.0, gbuf[:, sl],
                                      ALU.bypass, ALU.mult,
                                      wait=[("t", t_need), ("s", pb.n("s"))],
                                      inc=("v", 1))

        def c_acc(og, _t, ps, t_need):
            sl = slice(og * W, (og + 1) * W)
            pb.V.tensor_add(acc[:, sl], acc[:, sl], ps[:, 0:W],
                            wait=[("t", t_need)], inc=("v", 1))

        pb.G.wait("t", pb._wfree[0])
        pb.G.dma_start(out=pb.warena[:, 0:64], in_=routerT[0:128, 0:64],
                       inc=("lw", 16))
        for kc in range(8):
            pb.T.matmul(pb.ps_s1[0:8, 0:W], pb.warena[:, kc * 8:(kc + 1) * 8],
                        rhs_x(kc, 0), start=(kc == 0), stop=(kc == 7),
                        wait=[("lw", pb.n("lw")), ("v", pb.n("v"))] if kc == 0 else None,
                        inc=("t", 1) if kc == 7 else None)
        pb._wfree[0] = pb.n("t")
        lg = pb.tmpa
        pb.S.activation(out=lg[0:8, 0:W], in_=pb.ps_s1[0:8, 0:W], func=AF.Sigmoid,
                        bias=rb[0:8, :], wait=[("t", pb.n("t"))], inc=("s", 1))
        pm = pb.tmpb
        for j in range(4):
            bankt = pb.pst if j % 2 == 0 else pb.pst2
            pb.T.transpose(bankt.bitcast(F32)[0:128, 0:8],
                           lg[0:8, j * 128:(j + 1) * 128], idm[0:8, 0:8],
                           wait=[("s", pb.n("s")), ("v", pb.n("v"))],
                           inc=("t", 1))
            pb.V.tensor_copy(out=pm[:, j * 8:(j + 1) * 8],
                             in_=bankt.bitcast(F32)[0:128, 0:8],
                             wait=[("t", pb.n("t"))], inc=("v", 1))
        cw = pb.sqar[:, 0:512]
        m1 = pb.sqar[:, 512:640]
        m2 = pb.sqar[:, 640:768]
        e1 = pb.sqar[:, 768:1024]
        for j in range(4):
            pj = pm[:, j * 8:(j + 1) * 8]
            pb.V.wait("v")
            pb.V.tensor_reduce(m1[:, 0:1], pj, AX.X, ALU.max, inc=("v", 1))
            pb.V.wait("v")
            pb.V.tensor_single_scalar(e1[:, 0:8], pj, m1[:, 0:1], ALU.is_equal,
                                      inc=("v", 1))
            pb.V.wait("v")
            pb.V.scalar_tensor_tensor(e1[:, 8:16], e1[:, 0:8], -1e30, pj,
                                      ALU.mult, ALU.add, inc=("v", 1))
            pb.V.wait("v")
            pb.V.tensor_reduce(m2[:, 0:1], e1[:, 8:16], AX.X, ALU.max, inc=("v", 1))
            pb.V.wait("v")
            pb.V.tensor_single_scalar(e1[:, 16:24], pj, m2[:, 0:1], ALU.is_equal,
                                      inc=("v", 1))
            pb.V.wait("v")
            pb.V.tensor_add(m2[:, 1:2], m1[:, 0:1], m2[:, 0:1], inc=("v", 1))
            pb.V.wait("v")
            pb.V.reciprocal(m2[:, 2:3], m2[:, 1:2], inc=("v", 1))
            pb.V.wait("v")
            pb.V.tensor_mul(m1[:, 1:2], m1[:, 0:1], m2[:, 2:3])
            pb.V.tensor_mul(m2[:, 3:4], m2[:, 0:1], m2[:, 2:3], inc=("v", 1))
            pb.V.wait("v")
            pb.V.tensor_scalar_mul(e1[:, 0:8], e1[:, 0:8], m1[:, 1:2])
            pb.V.tensor_scalar_mul(e1[:, 16:24], e1[:, 16:24], m2[:, 3:4],
                                   inc=("v", 1))
            pb.V.wait("v")
            pb.V.tensor_add(cw[:, j * 8:(j + 1) * 8], e1[:, 0:8], e1[:, 16:24],
                            inc=("v", 1))
        cwT = pb.tmpa
        pb.V.wait("v")
        for j in range(4):
            bankt = pb.pst if j % 2 == 0 else pb.pst2
            pb.T.transpose(bankt.bitcast(F32)[0:8, 0:128], cw[:, j * 8:(j + 1) * 8],
                           idm[:, :], wait=[("v", pb.n("v")), ("s", pb.n("s"))],
                           inc=("t", 1))
            pb.S.copy(out=cwT[0:8, j * 128:(j + 1) * 128],
                      in_=bankt.bitcast(F32)[0:8, 0:128],
                      wait=[("t", pb.n("t"))], inc=("s", 1))
        mm(pb, ("til", sgT, 8), 1024, 2048, rhs_x, W, c_silu, free_sem="s")
        pb.T.wait("s")
        mm(pb, ("til", suT, 8), 1024, 2048, rhs_x, W, c_mulu, free_sem="v")
        pb.T.wait("v")
        mm(pb, ("til", sdT, 16), 2048, 1024, rhs_g, W, c_acc, free_sem="v")
        pb.T.wait("v")
        for e in range(8):
            mm(pb, ("til", gwT[e], 8), 1024, 2048, rhs_x, W, c_silu, free_sem="s")
            pb.T.wait("s")
            mm(pb, ("til", uwT[e], 8), 1024, 2048, rhs_x, W, c_mulu, free_sem="v")
            pb.T.wait("v")
            pb.T.matmul(pb.ps_s1[:, 0:W], sel8[0:8, e * 128:(e + 1) * 128],
                        cwT[0:8, 0:W], start=True, stop=True,
                        wait=[("s", pb.n("s"))], inc=("t", 1))
            for og in range(16):
                sl = slice(og * W, (og + 1) * W)
                pb.V.tensor_mul(gbuf[:, sl], gbuf[:, sl], pb.ps_s1[:, 0:W],
                                wait=[("t", pb.n("t"))] if og == 0 else None,
                                inc=("v", 1) if og == 15 else None)
            pb.T.wait("v")
            mm(pb, ("til", dwT[e], 16), 2048, 1024, rhs_g, W, c_acc, free_sem="v")
            pb.T.wait("v")
        emit_ln_to(pb, acc, None, add_to=xres, w=W, Tl=W, t0=0, span=W,
                   dTl=TL, dt0=tt * W)
        pb.barrier()


def _init_l2a(pb):
    pb._kv_free = [0, 0]
    pb._l2a_free = 0


def emit_l2attn(pb, xres, WqT, WkT, WvT, WoT, bq, bk, bv, bo, hbsel, idm,
                xn, fkbuf, v2buf, qbuf, outbuf, acc, cc_in, cc_out, groups,
                Tl=TL):
    NH = 16
    _init_l2a(pb)
    for tt in range(Tl // 512):
        emit_ln_to(pb, xres, xn, w=512, t0=tt * 512, span=512)
    pb.barrier(consumers=("tensor",), producers=["v"])
    rhs_x = lambda kc, tt: xn[:, kc * Tl + tt * 512: kc * Tl + tt * 512 + 512]

    sq_bf = pb.sqar[:, 1024:1536].bitcast(BF16)
    pb._tt_free = [0, 0]

    def phi_ops(ps, bias, og, dst, sl):
        pb.V.tensor_scalar(pb.tmpa[:], ps[:, 0:512], bias[:, og:og + 1], 0.0,
                           ALU.add, ALU.min, wait=[("s", pb.n("s"))])
        pb.V.tensor_scalar(dst[:, sl], ps[:, 0:512], bias[:, og:og + 1], 0.0,
                           ALU.add, ALU.max, inc=("v", 1))
        pb.S.activation(out=pb.tmpb[:], in_=pb.tmpa[:], func=AF.Exp,
                        wait=[("v", pb.n("v"))], inc=("s", 1))
        pb.V.tensor_add(dst[:, sl], dst[:, sl], pb.tmpb[:],
                        wait=[("s", pb.n("s"))], inc=("v", 1))

    def tok_tiles(ogo, bias, dst, hstep, do_phi):
        def c(_og, tt, ps, t_need):
            if do_phi:
                pb.V.wait("t", t_need)
                phi_ops(ps, bias, ogo, sq_bf, slice(0, 512))
            else:
                pb.S.activation(out=sq_bf[:, 0:512], in_=ps[:, 0:512],
                                func=AF.Identity, bias=bias[:, ogo:ogo + 1],
                                wait=[("t", t_need)], inc=("s", 1))
            for j in range(4):
                cglob = tt * 4 + j
                bankt = pb.pst if j % 2 == 0 else pb.pst2
                wt_ = ([("v", pb.n("v")), ("s", pb.n("s"))] if j < 2
                       else [("v", pb._tt_free[j % 2])])
                pb.T.transpose(bankt[0:128, 0:128],
                               sq_bf[:, j * 128:(j + 1) * 128], idm[:, :],
                               wait=wt_, inc=("t", 1))
                for half in range(2):
                    pb.V.tensor_copy(out=dst[:, (half * (Tl // 128) + cglob) * hstep:
                                             (half * (Tl // 128) + cglob) * hstep + 64],
                                     in_=bankt[0:128, half * 64:half * 64 + 64],
                                     wait=[("t", pb.n("t"))] if half == 0 else None,
                                     inc=("v", 1) if half == 1 else None)
                pb._tt_free[j % 2] = pb.n("v")
        return c

    kvs = pb.sqar
    for og in range(8):
        mm(pb, ("til", WkT[og * 128:(og + 1) * 128, :], 8), 1024, 128, rhs_x,
           Tl, tok_tiles(og, bk, fkbuf, 64, True), free_sem="v")
        pb.T.wait("v")
        pb.T.wait("s")
        mm(pb, ("til", WvT[og * 128:(og + 1) * 128, :], 8), 1024, 128, rhs_x,
           Tl, tok_tiles(og, bv, v2buf, 65, False), free_sem="v")
        pb.T.wait("v")
        pb.T.wait("s")
        pb.V.wait("v")
        for hl in range(2):
            for c in range(Tl // 128):
                pb.V.memset(v2buf[:, (hl * (Tl // 128) + c) * 65 + 64:
                                   (hl * (Tl // 128) + c) * 65 + 65], 1.0,
                            inc=("v", 1) if (hl == 1 and c == Tl // 128 - 1)
                            else None)
        pb.barrier(consumers=("tensor",), producers=["v"])
        for hl in range(2):
            h = 2 * og + hl
            bank = pb.ps[h % 2]
            for c in range(Tl // 128):
                pb.T.matmul(bank[0:64, 0:65],
                            fkbuf[:, (hl * (Tl // 128) + c) * 64:
                                  (hl * (Tl // 128) + c) * 64 + 64],
                            v2buf[:, (hl * (Tl // 128) + c) * 65:
                                  (hl * (Tl // 128) + c) * 65 + 65],
                            start=(c == 0), stop=(c == Tl // 128 - 1),
                            wait=[("v", pb._kv_free[h % 2])] if c == 0 else None,
                            inc=("t", 1) if c == Tl // 128 - 1 else None)
            pb.V.tensor_copy(out=kvs[0:64, h * 65:(h + 1) * 65],
                             in_=bank[0:64, 0:65],
                             wait=[("t", pb.n("t"))], inc=("v", 1))
            pb._kv_free[h % 2] = pb.n("v")
        pb.barrier(consumers=("vector",), producers=["t"])
    pb.Y.dma_start(out=cc_in[0:64, :], in_=kvs[0:64, 0:NH * 65],
                   wait=[("v", pb.n("v"))], inc=("st", 16))
    pb.G.wait("st", pb.n("st"))
    pb.G.collective_compute("AllReduce", ALU.add, replica_groups=groups,
                            ins=[cc_in.opt()], outs=[cc_out.opt()], inc=("cc", 1))
    pb.barrier()
    def c_q(og, tt, ps, t_need):
        pb.V.wait("t", t_need)
        phi_ops(ps, bq, og, qbuf,
                slice(og * Tl + tt * 512, og * Tl + tt * 512 + 512))
    mm(pb, ("til", WqT, 8), 1024, 1024, rhs_x, Tl, c_q, free_sem="v")
    pb.T.wait("v")
    kv2 = pb.sqar[:, 0:128]
    kv2b = pb.sqar[:, 128:256]
    dn2 = pb.sqar[:, 256:260]
    kv2bf = pb.sqar[:, 512:576].bitcast(BF16)
    dn2bf = pb.sqar[:, 576:577].bitcast(BF16)
    for hp in range(8):
        hA, hB = 2 * hp, 2 * hp + 1
        pb.V.wait("t")
        pb.V.memset(kv2[:, 0:128], 0.0)
        pb.V.memset(dn2[:, 0:2], 0.0, inc=("v", 1))
        pb.Y.dma_start(out=kv2.bitcast(F32)[0:64, 0:64]
                       if False else kv2[0:64, 0:64],
                       in_=cc_out[0:64, hA * 65: hA * 65 + 64],
                       wait=[("v", pb.n("v")), ("cc", pb.n("cc"))], inc=("ld", 16))
        pb.Y.dma_start(out=kv2[64:128, 64:128],
                       in_=cc_out[0:64, hB * 65: hB * 65 + 64], inc=("ld", 16))
        pb.Y.dma_start(out=dn2[0:64, 0:1],
                       in_=cc_out[0:64, hA * 65 + 64: hA * 65 + 65], inc=("ld", 16),
                       allow_slow_non_contiguous=True)
        pb.Y.dma_start(out=dn2[64:128, 1:2],
                       in_=cc_out[0:64, hB * 65 + 64: hB * 65 + 65], inc=("ld", 16),
                       allow_slow_non_contiguous=True)
        pb.V.tensor_copy(out=kv2bf[:, 0:128], in_=kv2[:, 0:128],
                         wait=[("ld", pb.n("ld"))])
        pb.V.tensor_copy(out=dn2bf[:, 0:2], in_=dn2[:, 0:2], inc=("v", 1))
        for tt in range(Tl // 512):
            qsl = slice(hp * Tl + tt * 512, hp * Tl + tt * 512 + 512)
            pb.T.matmul(pb.ps[2][:, 0:512], kv2bf[:, 0:128], qbuf[:, qsl],
                        start=True, stop=True,
                        wait=[("v", pb.n("v")), ("s", pb._l2a_free)])
            pb.T.matmul(pb.ps_s1[0:2, 0:512], dn2bf[:, 0:2], qbuf[:, qsl],
                        start=True, stop=True, inc=("t", 1))
            pb.V.tensor_scalar_max(pb.tmpa[0:2, 0:512], pb.ps_s1[0:2, 0:512],
                                   1e-6, wait=[("t", pb.n("t"))], inc=("v", 1))
            pb.V.wait("v")
            pb.V.reciprocal(pb.tmpa[0:2, 0:512], pb.tmpa[0:2, 0:512], inc=("v", 1))
            pb.T.matmul(pb.ps_s2[:, 0:512], hbsel[0:2, :], pb.tmpa[0:2, 0:512],
                        start=True, stop=True, wait=[("v", pb.n("v"))],
                        inc=("t", 1))
            pb.S.copy(out=pb.tmpb[:, 0:512], in_=pb.ps[2][:, 0:512],
                      wait=[("t", pb.n("t"))], inc=("s", 1))
            pb.V.tensor_mul(outbuf[:, qsl], pb.tmpb[:, 0:512], pb.ps_s2[:, 0:512],
                            wait=[("s", pb.n("s"))], inc=("v", 1))
            pb._l2a_free = pb.n("s")
    pb.barrier()
    for tt in range(Tl // 512):
        rhs_o = lambda kc, _tt: outbuf[:, kc * Tl + tt * 512: kc * Tl + tt * 512 + 512]
        def c_o(og, _t, ps, t_need):
            pb.S.activation(out=acc[:, og * 512:(og + 1) * 512], in_=ps[:, 0:512],
                            func=AF.Identity, bias=bo[:, og:og + 1],
                            wait=[("t", t_need)], inc=("s", 1))
        mm(pb, ("til", WoT, 8), 1024, 1024, rhs_o, 512, c_o, free_sem="s")
        pb.T.wait("s")
        pb.barrier(consumers=("vector",), producers=["s"])
        emit_ln_to(pb, acc, None, add_to=xres, w=512, Tl=512, t0=0, span=512,
                   dTl=Tl, dt0=tt * 512)
        pb.barrier()


def emit_flash(pb, k4, v2, hq, h0, nh, negB, cmask, sink, Tq=4096, scale=0.125):
    qtiles = [pb.tmpa.bitcast(BF16)[0:64, 0:512], pb.tmpb.bitcast(BF16)[0:64, 0:512]]
    ptile = [pb.sqar[:, 1024:1280].bitcast(BF16), pb.sqar[:, 1280:1536].bitcast(BF16)]
    osb = pb.sqar[:, 0:512]
    osb2 = pb.sqar[:, 1536:1792].bitcast(BF16)
    nck = Tq // 128
    if not hasattr(pb, "_fl"):
        pb._fl = {"q": [0, 0], "p": [0, 0], "s": [0, 0], "o": 0, "ost": 0}
    F = pb._fl
    for lh in range(nh):
        for tqb in range(Tq // 512):
            qb = (lh * (Tq // 512) + tqb) % 2
            pb.Y.dma_start(out=qtiles[qb], in_=hq[(h0 + lh) * 64:(h0 + lh) * 64 + 64,
                                                  tqb * 512:(tqb + 1) * 512],
                           wait=[("t", F["q"][qb])], inc=("ld", 16))
            ldq = pb.n("ld")
            ntk = (tqb + 1) * 4
            for tkc in range(ntk):
                bk = tkc % 2
                wm = [("s", F["s"][bk])]
                if tkc == 0:
                    wm.append(("ld", ldq))
                pb.T.matmul(pb.ps[bk][:, 0:512],
                            k4[0:64, lh * Tq + tkc * 128: lh * Tq + tkc * 128 + 128],
                            qtiles[qb], start=True, stop=True, wait=wm,
                            inc=("t", 1))
                if tkc >= tqb * 4:
                    d = tkc - tqb * 4
                    pb.V.tensor_add(pb.ps[bk][:, 0:512], pb.ps[bk][:, 0:512],
                                    cmask[:, d * 512:(d + 1) * 512],
                                    wait=[("t", pb.n("t"))], inc=("v", 1))
                    we = [("v", pb.n("v")), ("t", F["p"][bk])]
                else:
                    we = [("t", pb.n("t")), ("t", F["p"][bk])]
                pb.S.activation(out=ptile[bk], in_=pb.ps[bk][:, 0:512], func=AF.Exp,
                                scale=scale, bias=negB[:, 0:1], wait=we,
                                inc=("s", 1))
                F["s"][bk] = pb.n("s")
                pb.T.matmul(pb.ps[2][0:65, 0:512],
                            v2[:, (lh * nck + tkc) * 65:(lh * nck + tkc) * 65 + 65],
                            ptile[bk], start=(tkc == 0), stop=(tkc == ntk - 1),
                            wait=[("s", pb.n("s")), ("v", F["o"])],
                            inc=("t", 1))
                F["p"][bk] = pb.n("t")
            F["q"][qb] = pb.n("t")
            pb.V.reciprocal(pb.strow[:, 0:512], pb.ps[2][64:65, 0:512],
                            wait=[("t", pb.n("t")), ("s", pb.n("s"))], inc=("v", 1))
            pb.T.matmul(pb.ps_s1[0:64, 0:512], pb.ones_r[:, 0:64],
                        pb.strow[:, 0:512], start=True, stop=True,
                        wait=[("v", pb.n("v"))], inc=("t", 1))
            pb.S.copy(out=osb[0:64, :], in_=pb.ps[2][0:64, 0:512],
                      wait=[("t", pb.n("t"))], inc=("s", 1))
            pb.V.tensor_mul(osb2[0:64, 0:512], osb[0:64, :], pb.ps_s1[0:64, 0:512],
                            wait=[("s", pb.n("s")), ("st", F["ost"])], inc=("v", 1))
            F["o"] = pb.n("v")
            sink(h0 + lh, tqb, osb2[0:64, 0:512])
            F["ost"] = pb.n("st")


def bound_track(pb, Mq, ps, sqv):
    pb.S.activation(out=sqv[:], in_=ps[:, 0:512], func=AF.Square, inc=("s", 1))
    pb.T.matmul(pb.ps_s2[0:1, 0:512], pb.ones_c[:, :], sqv[:],
                start=True, stop=True, wait=[("s", pb.n("s")), ("v", pb.n("v"))],
                inc=("t", 1))
    pb.V.tensor_reduce(Mq[:, 1:2], pb.ps_s2[0:1, 0:512], AX.X, ALU.max,
                       wait=[("t", pb.n("t"))], inc=("v", 1))
    pb.V.wait("v")
    pb.V.tensor_max(Mq[:, 0:1], Mq[:, 0:1], Mq[:, 1:2], inc=("v", 1))


def bound_finish(pb, Mq, Mk, negB, scale_inv=0.125):
    pb.V.wait("v")
    pb.V.tensor_mul(Mq[:, 1:2], Mq[:, 0:1], Mk[:, 0:1], inc=("v", 1))
    pb.S.activation(out=Mq[:, 1:2], in_=Mq[:, 1:2], func=AF.Ln,
                    wait=[("v", pb.n("v"))], inc=("s", 1))
    pb.S.activation(out=Mq[:, 1:2], in_=Mq[:, 1:2], func=AF.Exp, scale=0.5,
                    wait=[("s", pb.n("s"))], inc=("s", 1))
    pb.V.tensor_scalar_mul(Mq[:, 1:2], Mq[:, 1:2], -float(scale_inv),
                           wait=[("s", pb.n("s"))], inc=("v", 1))
    pb.T.matmul(pb.ps_s2[:, 0:1], pb.ones_r[:, :], Mq[:, 1:2],
                start=True, stop=True, wait=[("v", pb.n("v"))], inc=("t", 1))
    pb.V.tensor_copy(out=negB[:, 0:1], in_=pb.ps_s2[:, 0:1],
                     wait=[("t", pb.n("t"))], inc=("v", 1))


def emit_top2(pb, lg_ps, W, idm, mode, rb=None):
    lg = pb.tmpb
    if mode == "sigmoid_norm":
        pb.S.activation(out=lg[0:8, 0:W], in_=lg_ps[0:8, 0:W], func=AF.Sigmoid,
                        bias=rb[0:8, :], wait=[("t", pb.n("t"))], inc=("s", 1))
    else:
        pb.S.copy(out=lg[0:8, 0:W], in_=lg_ps[0:8, 0:W],
                  wait=[("t", pb.n("t"))], inc=("s", 1))
    pm = pb.sqar[:, 0:32]
    cw = pb.sqar[:, 32:64]
    m1 = pb.sqar[:, 64:72]
    e1 = pb.sqar[:, 72:96]
    for j in range(W // 128):
        bankt = pb.pst if j % 2 == 0 else pb.pst2
        pb.T.transpose(bankt.bitcast(F32)[0:128, 0:8],
                       lg[0:8, j * 128:(j + 1) * 128], idm[0:8, 0:8],
                       wait=[("s", pb.n("s")), ("v", pb.n("v"))], inc=("t", 1))
        pb.V.tensor_copy(out=pm[:, j * 8:(j + 1) * 8],
                         in_=bankt.bitcast(F32)[0:128, 0:8],
                         wait=[("t", pb.n("t"))], inc=("v", 1))
    for j in range(W // 128):
        pj = pm[:, j * 8:(j + 1) * 8]
        pb.V.wait("v")
        pb.V.tensor_reduce(m1[:, 0:1], pj, AX.X, ALU.max, inc=("v", 1))
        pb.V.wait("v")
        pb.V.tensor_single_scalar(e1[:, 0:8], pj, m1[:, 0:1], ALU.is_equal,
                                  inc=("v", 1))
        pb.V.wait("v")
        pb.V.scalar_tensor_tensor(e1[:, 8:16], e1[:, 0:8], -1e30, pj,
                                  ALU.mult, ALU.add, inc=("v", 1))
        pb.V.wait("v")
        pb.V.tensor_reduce(m1[:, 1:2], e1[:, 8:16], AX.X, ALU.max, inc=("v", 1))
        pb.V.wait("v")
        pb.V.tensor_single_scalar(e1[:, 16:24], pj, m1[:, 1:2], ALU.is_equal,
                                  inc=("v", 1))
        if mode == "sigmoid_norm":
            pb.V.wait("v")
            pb.V.tensor_add(m1[:, 2:3], m1[:, 0:1], m1[:, 1:2], inc=("v", 1))
            pb.V.wait("v")
            pb.V.reciprocal(m1[:, 3:4], m1[:, 2:3], inc=("v", 1))
            pb.V.wait("v")
            pb.V.tensor_mul(m1[:, 4:5], m1[:, 0:1], m1[:, 3:4])
            pb.V.tensor_mul(m1[:, 5:6], m1[:, 1:2], m1[:, 3:4], inc=("v", 1))
        else:
            pb.V.wait("v")
            pb.V.tensor_sub(m1[:, 2:3], m1[:, 1:2], m1[:, 0:1], inc=("v", 1))
            pb.S.activation(out=m1[:, 3:4], in_=m1[:, 2:3], func=AF.Exp,
                            wait=[("v", pb.n("v"))], inc=("s", 1))
            pb.V.tensor_scalar_add(m1[:, 4:5], m1[:, 3:4], 1.0,
                                   wait=[("s", pb.n("s"))], inc=("v", 1))
            pb.V.wait("v")
            pb.V.reciprocal(m1[:, 4:5], m1[:, 4:5], inc=("v", 1))
            pb.V.wait("v")
            pb.V.tensor_mul(m1[:, 5:6], m1[:, 3:4], m1[:, 4:5], inc=("v", 1))
            pb.V.tensor_copy(out=m1[:, 4:5], in_=m1[:, 4:5])
        pb.V.wait("v")
        pb.V.tensor_scalar_mul(e1[:, 0:8], e1[:, 0:8], m1[:, 4:5])
        pb.V.tensor_scalar_mul(e1[:, 16:24], e1[:, 16:24], m1[:, 5:6],
                               inc=("v", 1))
        pb.V.wait("v")
        pb.V.tensor_add(cw[:, j * 8:(j + 1) * 8], e1[:, 0:8], e1[:, 16:24],
                        inc=("v", 1))
    cwT = pb.tmpa
    pb.V.wait("v")
    for j in range(W // 128):
        bankt = pb.pst if j % 2 == 0 else pb.pst2
        pb.T.transpose(bankt.bitcast(F32)[0:8, 0:128], cw[:, j * 8:(j + 1) * 8],
                       idm[:, :], wait=[("v", pb.n("v")), ("s", pb.n("s"))],
                       inc=("t", 1))
        pb.S.copy(out=cwT[0:8, j * 128:(j + 1) * 128],
                  in_=bankt.bitcast(F32)[0:8, 0:128],
                  wait=[("t", pb.n("t"))], inc=("s", 1))
    return cwT


def emit_l1ffn(pb, xres, g1t, g2t, WdkvT, WqTh, WukTh, WuvTh, WoT, gatewT,
               w1T, w2T, sel8, cmask, idm, idb, hbm, groups, A, Tl=TL,
               sel8_load=None):
    W = 512
    Tfull = 2 * Tl
    xn0t, xnrt = A["xt16f"], A["xt8b"]
    x3 = hbm["xn0"].rearrange("(c p) t -> c p t", p=128)
    a3 = hbm["agx_in"].rearrange("(c p) t -> c p t", p=128)
    for tt in range(Tl // W):
        emit_ln_to(pb, xres, xn0t, w=W, t0=tt * W, span=W, dTl=W, dt0=0)
        pb.Y.wait("v")
        for c in range(8):
            pb.Y.dma_start(out=x3[c, :, tt * W:(tt + 1) * W],
                           in_=xn0t[:, c * W:(c + 1) * W], inc=("st", 16))
        emit_ln_to(pb, xn0t, xnrt, w=W, Tl=W, rms=True, eps2=1e-6,
                   gain=(g1t, None))
        pb.Y.wait("v")
        for c in range(8):
            pb.Y.dma_start(out=a3[c, :, tt * W:(tt + 1) * W],
                           in_=xnrt[:, c * W:(c + 1) * W], inc=("st", 16))
    pb.G.wait("st")
    pb.G.collective_compute("AllGather", ALU.bypass, replica_groups=groups,
                            ins=[hbm["agx_in"].opt()], outs=[hbm["agx_out"].opt()],
                            inc=("cc", 1))
    pb.barrier()
    xnhalf = A["xnfull"]
    cbuf = A["cbuf"]
    Mq = pb.sqar[0:1, 2560:2562]
    Mk = pb.sqar[0:1, 2562:2564]
    sqv = pb.sqar[:, 2048:2560]
    pb.V.memset(pb.sqar[0:1, 2560:2564], 0.0, inc=("v", 1))
    qsb = pb.tmpb.bitcast(BF16)[:, 0:512]
    def mk_cproj(dst_hbm, Mx, row0, tof):
        def c_q(og, tt, ps, t_need):
            pb.S.activation(out=qsb[:, 0:W], in_=ps[:, 0:W], func=AF.Copy,
                            wait=[("t", t_need), ("st", pb.n("st"))], inc=("s", 1))
            pb.Y.dma_start(out=dst_hbm[row0 + og * 128: row0 + (og + 1) * 128,
                                       tof + tt * W: tof + (tt + 1) * W],
                           in_=qsb[:, 0:W], wait=[("s", pb.n("s"))], inc=("st", 16))
            bound_track(pb, Mx, ps, sqv)
        return c_q
    for hf in range(2):
        tof = hf * Tl
        for kc in range(8):
            pb.Y.dma_start(out=xnhalf[:, kc * Tl:(kc + 1) * Tl],
                           in_=hbm["agx_out"][hf * 1024 + kc * 128:
                                              hf * 1024 + kc * 128 + 128, :],
                           wait=[("t", pb.n("t")), ("s", pb.n("s")),
                                 ("cc", pb.n("cc"))],
                           inc=("ld", 16))
        pb.barrier()
        rhs_half = lambda kc, tt: xnhalf[:, kc * Tl + tt * W: kc * Tl + tt * W + W]
        def c_c(og, tt, ps, t_need, tof=tof):
            pb.S.activation(out=cbuf[:, og * Tfull + tof + tt * W:
                                     og * Tfull + tof + tt * W + W],
                            in_=ps[:, 0:W], func=AF.Copy, wait=[("t", t_need)],
                            inc=("s", 1))
        mm(pb, ("til", WdkvT, 8), 1024, 512, rhs_half, Tl, c_c, free_sem="s")
        pb.T.wait("s")
        mm(pb, ("til", WqTh, 8), 1024, 512, rhs_half, Tl,
           mk_cproj(hbm["hq"], Mq, 0, tof), free_sem="v")
        pb.T.wait("v")
        pb.barrier()
    rhs_c = lambda kc, tt: cbuf[:, kc * Tfull + tt * W: kc * Tfull + tt * W + W]
    if "negB" not in pb.tensors:
        pb.sbuf("negB", [128, 1], F32)
    negB = pb.tensors["negB"]
    sq_bf = pb.sqar[:, 1792:2048].bitcast(BF16)
    k4, v2 = A["k4"], A["v2"]
    for g in range(2):
        mm(pb, ("til", WukTh[g], 4), 512, 256, rhs_c, Tfull,
           mk_cproj(hbm["hk"], Mk, g * 256, 0), free_sem="v")
        pb.T.wait("v")
    bound_finish(pb, Mq, Mk, negB)
    for g in range(2):
        pb._tt_free = [0, 0]
        def c_v(og, tt, ps, t_need):
            pb.S.activation(out=sq_bf[:, 0:W], in_=ps[:, 0:W], func=AF.Copy,
                            wait=[("t", t_need), ("t", pb.n("t"))], inc=("s", 1))
            for j in range(4):
                cglob = tt * 4 + j
                bankt = pb.pst if j % 2 == 0 else pb.pst2
                wt_ = [("s", pb.n("s")), ("v", pb.n("v"))] if j < 2 else \
                      [("v", pb._tt_free[j % 2]), ("s", pb.n("s"))]
                pb.T.transpose(bankt[0:128, 0:128], sq_bf[:, j * 128:(j + 1) * 128],
                               idb[:, :], wait=wt_, inc=("t", 1))
                for half in range(2):
                    lh = og * 2 + half
                    pb.V.tensor_copy(
                        out=v2[:, (lh * 32 + cglob) * 65:(lh * 32 + cglob) * 65 + 64],
                        in_=bankt[0:128, half * 64:half * 64 + 64],
                        wait=[("t", pb.n("t"))] if half == 0 else None,
                        inc=("v", 1) if half == 1 else None)
                pb._tt_free[j % 2] = pb.n("v")
        mm(pb, ("til", WuvTh[g], 4), 512, 256, rhs_c, Tfull, c_v, free_sem="v")
        pb.T.wait("v")
        pb.V.wait("v")
        for lh in range(4):
            for c in range(32):
                pb.V.memset(v2[:, (lh * 32 + c) * 65 + 64:(lh * 32 + c) * 65 + 65],
                            1.0, inc=("v", 1) if (lh == 3 and c == 31) else None)
        for lh in range(4):
            h = g * 4 + lh
            pb.Y.wait("v")
            pb.Y.wait("st")
            pb.Y.dma_start(out=k4[0:64, lh * 4096:(lh + 1) * 4096],
                           in_=hbm["hk"][h * 64:(h + 1) * 64, :], inc=("ld", 16))
        pb.barrier()
        def sink(h, tqb, tile):
            pb.Y.dma_start(out=hbm["ago_in"][h * 64:(h + 1) * 64,
                                             tqb * W:(tqb + 1) * W],
                           in_=tile, wait=[("v", pb.n("v"))], inc=("st", 16))
        emit_flash(pb, k4, v2, hbm["hq"], g * 4, 4, negB, cmask, sink,
                   Tq=4096, scale=0.125)
        pb.barrier()
    pb.G.wait("st")
    pb.G.collective_compute("AllGather", ALU.bypass, replica_groups=groups,
                            ins=[hbm["ago_in"].opt()], outs=[hbm["ago_out"].opt()],
                            inc=("cc", 1))
    pb.barrier()
    if sel8_load is not None:
        pb.Y.dma_start(out=sel8[0:8, :], in_=sel8_load[:, :], inc=("ld", 16))
        pb.barrier(consumers=("tensor",), producers=["ld"])
    oful = A["oful"]
    x1 = A["xt16f"]
    xn2 = A["xt8b"]
    hid = A["hid"]
    acc = A["acc"]
    for tt in range(Tl // W):
        pb.G.wait("t")
        ofv = oful
        agout = hbm["ago_out"]
        def mkbr(base, tt=tt, ofv=ofv, agout=agout):
            def f(eng, pbx):
                for kc in range(8):
                    eng.dma_start(out=ofv[:, kc * W:(kc + 1) * W],
                                  in_=agout[kc * 128:(kc + 1) * 128,
                                            base + tt * W: base + (tt + 1) * W]
                                  ).then_inc(pbx.sems["lw"], 16)
            return f
        rank_switch(pb, "gpsimd", 2, [mkbr(0), mkbr(Tl)], incs=[("lw", 16 * 8)])
        pb.barrier(consumers=("tensor",), producers=["lw"])
        rhs_of = lambda kc, _tt: oful[:, kc * W:(kc + 1) * W]
        def c_wo(og, _t, ps, t_need):
            pb.Y.dma_start(out=pb.tmpa[:],
                           in_=x3[og, :, tt * W:(tt + 1) * W],
                           wait=[("v", pb.n("v"))], inc=("ld", 16))
            pb.V.tensor_add(x1[:, og * W:(og + 1) * W], ps[:, 0:W], pb.tmpa[:],
                            wait=[("t", t_need), ("ld", pb.n("ld"))],
                            inc=("v", 1))
        mm(pb, ("til", WoT, 8), 1024, 1024, rhs_of, W, c_wo, free_sem="v")
        pb.T.wait("v")
        pb.barrier(consumers=("tensor", "scalar"), producers=["v"])
        emit_ln_to(pb, x1, xn2, w=W, Tl=W, rms=True, eps2=1e-6, gain=(g2t, None))
        pb.barrier(consumers=("tensor",), producers=["v"])
        rhs_x2 = lambda kc, _tt: xn2[:, kc * W:(kc + 1) * W]
        pb.G.wait("t", pb._wfree[0])
        pb.G.dma_start(out=pb.warena[:, 0:64], in_=gatewT[0:128, 0:64],
                       inc=("lw", 16))
        for kc in range(8):
            pb.T.matmul(pb.ps_s1[0:8, 0:W], pb.warena[:, kc * 8:(kc + 1) * 8],
                        rhs_x2(kc, 0), start=(kc == 0), stop=(kc == 7),
                        wait=[("lw", pb.n("lw")), ("v", pb.n("v")),
                              ("s", pb.n("s"))] if kc == 0 else None,
                        inc=("t", 1) if kc == 7 else None)
        pb._wfree[0] = pb.n("t")
        cwT = emit_top2(pb, pb.ps_s1, W, idm, "softmax")
        pb.V.memset(acc[:, :], 0.0, inc=("v", 1))
        for e in range(8):
            def c_h(og, _t, ps, t_need):
                pb.S.activation(out=hid[:, og * W:(og + 1) * W], in_=ps[:, 0:W],
                                func=AF.Silu, wait=[("t", t_need)], inc=("s", 1))
            mm(pb, ("til", w1T[e], 8), 1024, 2048, rhs_x2, W, c_h, free_sem="s")
            pb.T.wait("s")
            pb.T.matmul(pb.ps_s2[:, 0:W], sel8[0:8, e * 128:(e + 1) * 128],
                        cwT[0:8, 0:W], start=True, stop=True,
                        wait=[("s", pb.n("s")), ("v", pb.n("v"))], inc=("t", 1))
            for og in range(16):
                sl = slice(og * W, (og + 1) * W)
                pb.V.tensor_mul(hid[:, sl], hid[:, sl], pb.ps_s2[:, 0:W],
                                wait=[("t", pb.n("t"))] if og == 0 else None,
                                inc=("v", 1) if og == 15 else None)
            pb.T.wait("v")
            rhs_h = lambda kc, _tt: hid[:, kc * W:(kc + 1) * W]
            def c_acc(og, _t, ps, t_need):
                sl = slice(og * W, (og + 1) * W)
                pb.V.tensor_add(acc[:, sl], acc[:, sl], ps[:, 0:W],
                                wait=[("t", t_need)], inc=("v", 1))
            mm(pb, ("til", w2T[e], 16), 2048, 1024, rhs_h, W, c_acc, free_sem="v")
            pb.T.wait("v")
        for og in range(8):
            sl = slice(og * W, (og + 1) * W)
            pb.V.tensor_add(acc[:, sl], acc[:, sl], x1[:, sl],
                            inc=("v", 1) if og == 7 else None)
        pb.barrier(consumers=("tensor", "scalar"), producers=["v"])
        emit_ln_to(pb, acc, None, add_to=xres, w=W, Tl=W, t0=0, span=W,
                   dTl=Tl, dt0=tt * W)
        pb.barrier()


def emit_softplus(pb, dst, src, neg=False, w=512):
    np_ = src.shape[0]
    ta = pb.spa[0:np_, 0:w]
    tb = pb.spb[0:np_, 0:w]
    pb.S.activation(out=ta, in_=src, func=AF.Abs,
                    wait=[("v", pb.n("v"))], inc=("s", 1))
    pb.S.activation(out=ta, in_=ta, func=AF.Exp,
                    scale=-1.0, wait=[("s", pb.n("s"))], inc=("s", 1))
    pb.S.activation(out=ta, in_=ta, func=AF.Ln,
                    bias=pb.cf(1.0)[0:np_, :], wait=[("s", pb.n("s"))], inc=("s", 1))
    pb.V.tensor_scalar_max(tb, src, 0.0, wait=[("s", pb.n("s"))])
    if neg:
        pb.V.tensor_add(tb, tb, ta)
        pb.V.tensor_scalar_mul(dst, tb, -1.0, inc=("v", 1))
    else:
        pb.V.tensor_add(dst, tb, ta, inc=("v", 1))


def emit_l0attn(pb, xres, Wt, vt, cmask, idm, idb, hbsel, hselc, perm,
                ropeCd, ropeSd, hbm, groups, A, Tl=TL, stop_after=None):
    W = 512
    T4 = 4096
    xn0t, xnrt = A["xt16f"], A["xt8b"]
    x3 = hbm["xn0"].rearrange("(c p) t -> c p t", p=128)
    a3 = hbm["agx_in"].rearrange("(c p) t -> c p t", p=128)
    for tt in range(Tl // W):
        emit_ln_to(pb, xres, xn0t, w=W, t0=tt * W, span=W, dTl=W, dt0=0)
        pb.Y.wait("v")
        for c in range(8):
            pb.Y.dma_start(out=x3[c, :, tt * W:(tt + 1) * W],
                           in_=xn0t[:, c * W:(c + 1) * W], inc=("st", 16))
        emit_ln_to(pb, xn0t, xnrt, w=W, Tl=W, gain=(vt["ln1g"], vt["ln1b"]))
        pb.Y.wait("v")
        for c in range(8):
            pb.Y.dma_start(out=a3[c, :, tt * W:(tt + 1) * W],
                           in_=xnrt[:, c * W:(c + 1) * W], inc=("st", 16))
    pb.G.wait("st")
    pb.G.collective_compute("AllGather", ALU.bypass, replica_groups=groups,
                            ins=[hbm["agx_in"].opt()], outs=[hbm["agx_out"].opt()],
                            inc=("cc", 1))
    pb.barrier()
    if stop_after == 'P0':
        pb.barrier()
        return
    xnhalf = A["xnfull"]
    qsb = pb.tmpb.bitcast(BF16)[:, 0:512]
    for hf in range(2):
        tof = hf * Tl
        for kc in range(8):
            pb.Y.dma_start(out=xnhalf[:, kc * Tl:(kc + 1) * Tl],
                           in_=hbm["agx_out"][hf * 1024 + kc * 128:
                                              hf * 1024 + kc * 128 + 128, :],
                           wait=[("t", pb.n("t")), ("s", pb.n("s")),
                                 ("cc", pb.n("cc")), ("st", pb.n("st"))],
                           inc=("ld", 16))
        pb.barrier()
        rhs_half = lambda kc, tt: xnhalf[:, kc * Tl + tt * W: kc * Tl + tt * W + W]
        def mk_proj(dst_hbm, bias, tof):
            def c(og, tt, ps, t_need):
                if bias is None:
                    pb.S.activation(out=qsb[:, 0:W], in_=ps[:, 0:W], func=AF.Copy,
                                    wait=[("t", t_need), ("st", pb.n("st"))],
                                    inc=("s", 1))
                else:
                    pb.S.activation(out=qsb[:, 0:W], in_=ps[:, 0:W],
                                    func=AF.Identity, bias=bias[:, og:og + 1],
                                    wait=[("t", t_need), ("st", pb.n("st"))],
                                    inc=("s", 1))
                pb.Y.dma_start(out=dst_hbm[og * 128:(og + 1) * 128,
                                           tof + tt * W: tof + (tt + 1) * W],
                               in_=qsb[:, 0:W], wait=[("s", pb.n("s"))],
                               inc=("st", 16))
            return c
        for (wname, dst, bias) in (("WqTh", "hbq", None), ("WkTh", "hbk", None),
                                   ("WvTh", "hbv", None),
                                   ("gateTh", "hbg", vt["gateb"]),
                                   ("pTh", "hbp", vt["pb_"])):
            mm(pb, ("til", Wt[wname], 8), 1024, 512, rhs_half, Tl,
               mk_proj(hbm[dst], bias, tof), free_sem="s")
            pb.T.wait("s")
        pb.G.wait("t", pb._wfree[0])
        pb.G.dma_start(out=pb.warena[:, 0:64], in_=Wt["clockTh"][0:128, 0:64],
                       inc=("lw", 16))
        for tt in range(Tl // W):
            for kc in range(8):
                pb.T.matmul(pb.ps_s1[0:8, 0:W], pb.warena[:, kc * 8:(kc + 1) * 8],
                            rhs_half(kc, tt), start=(kc == 0), stop=(kc == 7),
                            wait=[("lw", pb.n("lw")), ("v", pb.n("v")),
                                  ("s", pb.n("s")), ("st", pb.n("st"))] if kc == 0 else None,
                            inc=("t", 1) if kc == 7 else None)
            pb.S.activation(out=pb.tmpa[0:8, 0:W], in_=pb.ps_s1[0:8, 0:W],
                            func=AF.Identity, bias=vt["clockb"][0:8, :],
                            wait=[("t", pb.n("t"))], inc=("s", 1))
            emit_softplus(pb, pb.tmpa[0:8, 0:W], pb.tmpa[0:8, 0:W], w=W)
            pb.Y.dma_start(out=hbm["hclk"][0:8, tof + tt * W: tof + (tt + 1) * W],
                           in_=pb.tmpa[0:8, 0:W], wait=[("v", pb.n("v"))],
                           inc=("st", 16))
            pb.S.wait("st")
            pb.V.wait("st")
        pb._wfree[0] = pb.n("t")
        pb.barrier()
    if stop_after == 'P2':
        pb.barrier()
        return
    ropeC = pb.warena[:, 0:4096]
    ropeS = pb.warena[:, 4096:8192]
    pb.G.wait("t", pb._wfree[0]); pb.G.wait("t", pb._wfree[1])
    pb.G.dma_start(out=ropeC, in_=ropeCd[:, :], inc=("lw", 16))
    pb.G.dma_start(out=ropeS, in_=ropeSd[:, :], inc=("lw", 16))
    gcp, pex, pcs, gk = A["gp1"], A["gp2"], A["gp3"], A["gp4"]
    gin = A["gbq"]
    gbk = A["gbk"]
    swp = pb.sqar[:, 1536:1792].bitcast(BF16)
    pb._l0f = 0
    import os as _os
    _l0g = _os.environ.get("L0G", "")
    for hp in range(1 if _l0g else 4):
        rows = slice(hp * 128, (hp + 1) * 128)
        pb.Y.wait("v"); pb.Y.wait("s"); pb.Y.wait("st")
        pb.Y.dma_start(out=gin[:, 0:T4], in_=hbm["hbg"][rows, :], inc=("ld", 16))
        pb.V.wait("ld")
        pb.S.wait("ld")
        for c8 in range(T4 // W):
            sl = slice(c8 * W, (c8 + 1) * W)
            emit_softplus(pb, gcp[:, sl], gin[:, sl], neg=True, w=W)
        pb.V.wait("v")
        pb.V.tensor_tensor_scan(pex[:, 0:T4], gcp[:, 0:T4], gcp[:, 0:T4],
                                0.0, ALU.add, ALU.bypass, inc=("v", 1))
        pb.V.wait("v")
        pb.V.tensor_scalar(pex[:, 0:T4], pex[:, 0:T4], -60.0, 50.0,
                           ALU.max, ALU.min, inc=("v", 1))
        pb.S.activation(out=gcp[:, 0:T4], in_=pex[:, 0:T4], func=AF.Exp,
                        wait=[("v", pb.n("v"))], inc=("s", 1))
        if _l0g == "G1":
            continue
        pb.Y.dma_start(out=gin[:, 0:T4], in_=hbm["hbp"][rows, :],
                       wait=[("v", pb.n("v")), ("s", pb.n("s"))], inc=("ld", 16))
        pb.V.wait("ld")
        pb.S.wait("ld")
        pb.V.tensor_copy(out=pex[:, 0:T4], in_=gin[:, 0:T4], inc=("v", 1))
        pb.V.wait("v")
        pb.V.tensor_reduce(pb.tmpa[:, 0:1], pex[:, 0:T4], AX.X, ALU.max,
                           inc=("v", 1))
        pb.V.wait("v")
        pb.V.tensor_scalar_mul(pb.tmpa[:, 1:2], pb.tmpa[:, 0:1], -1.0,
                               inc=("v", 1))
        pb.S.activation(out=pex[:, 0:T4], in_=pex[:, 0:T4], func=AF.Exp,
                        bias=pb.tmpa[:, 1:2], wait=[("v", pb.n("v"))],
                        inc=("s", 1))
        pb.Y.dma_start(out=pb.sqar[0:2, 0:T4], in_=hbm["hclk"][2 * hp:2 * hp + 2, :],
                       wait=[("v", pb.n("v")), ("s", pb.n("s"))], inc=("ld", 16))
        for c8 in range(T4 // W):
            pb.T.matmul(pb.ps_s2[:, 0:W], hbsel[0:2, :],
                        pb.sqar[0:2, c8 * W:(c8 + 1) * W], start=True, stop=True,
                        wait=[("ld", pb.n("ld")), ("s", pb.n("s")),
                              ("v", pb._l0f)], inc=("t", 1))
            pb.V.tensor_mul(pex[:, c8 * W:(c8 + 1) * W],
                            pex[:, c8 * W:(c8 + 1) * W],
                            pb.ps_s2[:, 0:W], wait=[("t", pb.n("t"))], inc=("v", 1))
            pb._l0f = pb.n("v")
        pb.V.wait("v")
        pb.V.tensor_tensor_scan(pcs[:, 0:T4], pex[:, 0:T4], pex[:, 0:T4],
                                0.0, ALU.add, ALU.bypass, inc=("v", 1))
        pb.V.wait("v")
        pb.V.tensor_scalar_add(pcs[:, 0:T4], pcs[:, 0:T4], 1e-8, inc=("v", 1))
        pb.V.wait("v")
        pb.V.reciprocal(pcs[:, 0:T4], pcs[:, 0:T4], inc=("v", 1))
        if _l0g == "G2":
            continue
        for (srcb, dstb, isq) in (("hbq", "hq", True), ("hbk", "hk", False)):
            pb.Y.wait("v")
            pb.Y.dma_start(out=gin[:, 0:T4], in_=hbm[srcb][rows, :],
                           inc=("ld", 16))
            pb.V.wait("ld")
            pb.S.wait("ld")
            for c8 in range(T4 // W):
                sl = slice(c8 * W, (c8 + 1) * W)
                pb.S.activation(out=pb.tmpa[:, 0:W], in_=gin[:, sl],
                                func=AF.Square,
                                wait=[("v", pb.n("v")), ("t", pb.n("t"))],
                                inc=("s", 1))
                pb.T.matmul(pb.ps_s1[0:2, 0:W], hselc[:, 0:2], pb.tmpa[:, 0:W],
                            start=True, stop=True,
                            wait=[("s", pb.n("s")), ("v", pb.n("v"))],
                            inc=("t", 1))
                pb.V.tensor_scalar_max(pb.tmpa[0:2, 0:W], pb.ps_s1[0:2, 0:W],
                                       1e-24, wait=[("t", pb.n("t"))], inc=("v", 1))
                pb.S.activation(out=pb.tmpa[0:2, 0:W], in_=pb.tmpa[0:2, 0:W],
                                func=AF.Ln, wait=[("v", pb.n("v"))], inc=("s", 1))
                pb.S.activation(out=pb.tmpa[0:2, 0:W], in_=pb.tmpa[0:2, 0:W],
                                func=AF.Exp, scale=-0.5, wait=[("s", pb.n("s"))],
                                inc=("s", 1))
                pb.T.matmul(pb.ps_s2[:, 0:W], hbsel[0:2, :], pb.tmpa[0:2, 0:W],
                            start=True, stop=True,
                            wait=[("s", pb.n("s")), ("v", pb.n("v"))],
                            inc=("t", 1))
                pb.V.scalar_tensor_tensor(gbk[:, sl], gin[:, sl], 0.0,
                                          pb.ps_s2[:, 0:W], ALU.bypass, ALU.mult,
                                          wait=[("t", pb.n("t"))], inc=("v", 1))
            if _l0g == "G3":
                continue
            for c8 in range(T4 // W):
                sl = slice(c8 * W, (c8 + 1) * W)
                pb.T.matmul(pb.ps_s1[:, 0:W], perm[:, :], gbk[:, sl],
                            start=True, stop=True,
                            wait=[("v", pb.n("v")), ("s", pb.n("s"))],
                            inc=("t", 1))
                pb.S.copy(out=swp[:, 0:W], in_=pb.ps_s1[:, 0:W],
                          wait=[("t", pb.n("t"))], inc=("s", 1))
                pb.V.tensor_mul(pb.tmpa[:, 0:W], gbk[:, sl], ropeC[:, sl])
                pb.V.scalar_tensor_tensor(pb.tmpb[:, 0:W], swp[:, 0:W], 0.0,
                                          ropeS[:, sl], ALU.bypass, ALU.mult,
                                          wait=[("s", pb.n("s"))])
                pb.V.tensor_add(gk[:, sl], pb.tmpa[:, 0:W], pb.tmpb[:, 0:W],
                                inc=("v", 1))
            if _l0g == "G4":
                continue
            pb.V.wait("v")
            if isq:
                for c8 in range(T4 // W):
                    sl = slice(c8 * W, (c8 + 1) * W)
                    pb.V.tensor_mul(gk[:, sl], gk[:, sl], gcp[:, sl])
                    pb.V.tensor_mul(gk[:, sl], gk[:, sl], pcs[:, sl],
                                    inc=("v", 1))
            else:
                for c8 in range(T4 // W):
                    sl = slice(c8 * W, (c8 + 1) * W)
                    pb.V.tensor_scalar_add(pb.tmpa[:, 0:W], gcp[:, sl], 1e-8)
                    pb.V.reciprocal(pb.tmpa[:, 0:W], pb.tmpa[:, 0:W])
                    pb.V.tensor_mul(gk[:, sl], gk[:, sl], pb.tmpa[:, 0:W])
                    pb.V.tensor_mul(gk[:, sl], gk[:, sl], pex[:, sl],
                                    inc=("v", 1))
            pb.V.wait("v")
            for c8 in range(T4 // W):
                sl = slice(c8 * W, (c8 + 1) * W)
                pb.V.tensor_copy(out=gin[:, sl], in_=gk[:, sl], inc=("v", 1))
            pb.Y.wait("v")
            pb.Y.dma_start(out=hbm[dstb][rows, :], in_=gin[:, 0:T4],
                           inc=("st", 16))
        pb.barrier()
    if stop_after == 'P2b':
        pb.barrier()
        return
    k4, v2 = A["k4"], A["v2"]
    negB0 = pb.cf(0.0)
    for g in range(2):
        pb._tt_free = [0, 0]
        for lhp in range(2):
            hp = g * 2 + lhp
            pb.Y.wait("v")
            pb.Y.dma_start(out=gin[:, 0:T4], in_=hbm["hbv"][hp * 128:(hp + 1) * 128, :],
                           inc=("ld", 16))
            pb.G.wait("ld")
            for c in range(T4 // 128):
                bankt = pb.pst if c % 2 == 0 else pb.pst2
                wt_ = [("ld", pb.n("ld")), ("v", pb.n("v"))] if c < 2 else \
                      [("v", pb._tt_free[c % 2])]
                pb.T.transpose(bankt[0:128, 0:128], gin[:, c * 128:(c + 1) * 128],
                               idb[:, :], wait=wt_, inc=("t", 1))
                for half in range(2):
                    lh = lhp * 2 + half
                    pb.V.tensor_copy(
                        out=v2[:, (lh * 32 + c) * 65:(lh * 32 + c) * 65 + 64],
                        in_=bankt[0:128, half * 64:half * 64 + 64],
                        wait=[("t", pb.n("t"))] if half == 0 else None,
                        inc=("v", 1) if half == 1 else None)
                pb._tt_free[c % 2] = pb.n("v")
        pb.V.wait("v")
        for lh in range(4):
            for c in range(32):
                pb.V.memset(v2[:, (lh * 32 + c) * 65 + 64:(lh * 32 + c) * 65 + 65],
                            1.0, inc=("v", 1) if (lh == 3 and c == 31) else None)
        for lh in range(4):
            h = g * 4 + lh
            pb.Y.wait("v"); pb.Y.wait("st")
            pb.Y.dma_start(out=k4[0:64, lh * 4096:(lh + 1) * 4096],
                           in_=hbm["hk"][h * 64:(h + 1) * 64, :], inc=("ld", 16))
        pb.barrier()
        def sink(h, tqb, tile):
            pb.Y.dma_start(out=hbm["ago_in"][h * 64:(h + 1) * 64,
                                             tqb * W:(tqb + 1) * W],
                           in_=tile, wait=[("v", pb.n("v"))], inc=("st", 16))
        emit_flash(pb, k4, v2, hbm["hq"], g * 4, 4, negB0, cmask, sink,
                   Tq=4096, scale=0.125)
        pb.barrier()
    pb.G.wait("st")
    pb.G.collective_compute("AllGather", ALU.bypass, replica_groups=groups,
                            ins=[hbm["ago_in"].opt()], outs=[hbm["ago_out"].opt()],
                            inc=("cc", 1))
    pb.barrier()
    if stop_after == 'P3':
        pb.barrier()
        return
    oful = A["oful"]
    x1 = A["xt16f"]
    xn2 = A["xt8b"]
    hid = A["hid32"]
    acc = A["acc"]
    for tt in range(Tl // W):
        pb.G.wait("t")
        def mkbr(base, tt=tt):
            def f(eng, pbx):
                for kc in range(8):
                    eng.dma_start(out=oful[:, kc * W:(kc + 1) * W],
                                  in_=hbm["ago_out"][kc * 128:(kc + 1) * 128,
                                                     base + tt * W: base + (tt + 1) * W]
                                  ).then_inc(pbx.sems["lw"], 16)
            return f
        rank_switch(pb, "gpsimd", 2, [mkbr(0), mkbr(Tl)], incs=[("lw", 16 * 8)])
        pb.barrier(consumers=("tensor",), producers=["lw"])
        rhs_of = lambda kc, _tt: oful[:, kc * W:(kc + 1) * W]
        def c_wo(og, _t, ps, t_need):
            pb.Y.dma_start(out=pb.tmpa[:], in_=x3[og, :, tt * W:(tt + 1) * W],
                           wait=[("v", pb.n("v"))], inc=("ld", 16))
            pb.S.activation(out=pb.tmpb[:, 0:W], in_=ps[:, 0:W], func=AF.Identity,
                            bias=vt["outb"][:, og:og + 1],
                            wait=[("t", t_need), ("v", pb.n("v"))], inc=("s", 1))
            pb.V.tensor_add(x1[:, og * W:(og + 1) * W], pb.tmpb[:, 0:W], pb.tmpa[:],
                            wait=[("s", pb.n("s")), ("ld", pb.n("ld"))],
                            inc=("v", 1))
        mm(pb, ("til", Wt["out_wT"], 8), 1024, 1024, rhs_of, W, c_wo, free_sem="s")
        pb.T.wait("v")
        if _l0g == "P4a":
            continue
        pb.barrier(consumers=("tensor", "scalar"), producers=["v"])
        emit_ln_to(pb, x1, xn2, w=W, Tl=W, gain=(vt["ln2g"], vt["ln2b"]))
        pb.barrier(consumers=("tensor",), producers=["v"])
        rhs_x2 = lambda kc, _tt: xn2[:, kc * W:(kc + 1) * W]
        def c_h(og, _t, ps, t_need):
            pb.S.activation(out=hid[:, og * W:(og + 1) * W], in_=ps[:, 0:W],
                            func=AF.Gelu, bias=vt["m1b"][:, og:og + 1],
                            wait=[("t", t_need)], inc=("s", 1))
        mm(pb, ("til", Wt["mlp1T"], 8), 1024, 4096, rhs_x2, W, c_h, free_sem="s")
        pb.T.wait("s")
        if _l0g == "P4b":
            continue
        rhs_h = lambda kc, _tt: hid[:, kc * W:(kc + 1) * W]
        def c_o2(og, _t, ps, t_need):
            sl = slice(og * W, (og + 1) * W)
            pb.S.activation(out=pb.tmpb[:, 0:W], in_=ps[:, 0:W], func=AF.Identity,
                            bias=vt["m2b"][:, og:og + 1],
                            wait=[("t", t_need), ("v", pb.n("v"))], inc=("s", 1))
            pb.V.tensor_add(acc[:, sl], pb.tmpb[:, 0:W], x1[:, sl],
                            wait=[("s", pb.n("s"))], inc=("v", 1))
        mm(pb, ("til", Wt["mlp2T"], 32), 4096, 1024, rhs_h, W, c_o2, free_sem="v")
        pb.T.wait("v")
        if _l0g == "P4c":
            continue
        pb.barrier(consumers=("tensor", "scalar"), producers=["v"])
        if _l0g == "P4d":
            pb.barrier()
            continue
        emit_ln_to(pb, acc, None, add_to=xres, w=W, Tl=W, t0=0, span=W,
                   dTl=Tl, dt0=tt * W)
        pb.barrier()


def emit_l1attn(pb, xres, Wt, vt, hbm, groups, A, Tl=TL):
    pb._l1a_f = 0
    """Wt: WzT [8*128,8*128]; WxTh, WdtTh [4*128? ->(1024,512 slice)] tiled
    [4*128, 8*128]; WBT, WCT [128, 8*16] (router-style ow=16); WoutT
    [8*128, 8*128]; Alog [128, 64] (d-slice p-major x (dc,n)).
    vt: bz [128,8]; bxh [128,4]; bdth [128,4]; bBC [128,1] rows0:16 x2 cols;
    bout [128,8].
    hbm: xn0 f32; agx_in/out bf16; hz [1024, 2048] bf16; hdt, hxs [512, 4096]
    f32; hBC [32, 4096] f32; hy [512, 4096] bf16; ago_out [1024, 4096] bf16.
    A: xnloc bf16 [128, 8*2048] (alias ok); xnfull same; sc_a..sc_h f32
    [128, 2048] x6; yh f32 [128, 2048]; oful [128, 8*512]; xt16f; xt8b; acc.
    """
    W = 512
    Tfull = 2 * Tl
    xn0t = A["xt16f"]
    x3 = hbm["xn0"].rearrange("(c p) t -> c p t", p=128)
    a3 = hbm["agx_in"].rearrange("(c p) t -> c p t", p=128)
    for tt in range(Tl // W):
        emit_ln_to(pb, xres, xn0t, w=W, t0=tt * W, span=W, dTl=W, dt0=0)
        pb.Y.wait("v")
        for c in range(8):
            pb.Y.dma_start(out=x3[c, :, tt * W:(tt + 1) * W],
                           in_=xn0t[:, c * W:(c + 1) * W], inc=("st", 16))
        pb.V.wait("st")
        for c in range(8):
            pb.V.tensor_copy(out=A["xt8b"][:, c * W:(c + 1) * W],
                             in_=xn0t[:, c * W:(c + 1) * W],
                             inc=("v", 1) if c == 7 else None)
        pb.Y.wait("v")
        for c in range(8):
            pb.Y.dma_start(out=a3[c, :, tt * W:(tt + 1) * W],
                           in_=A["xt8b"][:, c * W:(c + 1) * W], inc=("st", 16))
    pb.G.wait("st")
    pb.G.collective_compute("AllGather", ALU.bypass, replica_groups=groups,
                            ins=[hbm["agx_in"].opt()], outs=[hbm["agx_out"].opt()],
                            inc=("cc", 1))
    pb.barrier()
    xnloc = A["xnloc"]
    for kc in range(8):
        pb.Y.dma_start(out=xnloc[:, kc * Tl:(kc + 1) * Tl],
                       in_=hbm["agx_in"][kc * 128:(kc + 1) * 128, :],
                       inc=("ld", 16))
    pb.barrier()
    rhs_loc = lambda kc, tt: xnloc[:, kc * Tl + tt * W: kc * Tl + tt * W + W]
    zsb = pb.tmpb.bitcast(BF16)[:, 0:512]
    hz3 = hbm["hz"].rearrange("(c p) t -> c p t", p=128)
    def c_z(og, tt, ps, t_need):
        pb.S.activation(out=zsb[:, 0:W], in_=ps[:, 0:W], func=AF.Sigmoid,
                        bias=vt["bz"][:, og:og + 1],
                        wait=[("t", t_need), ("st", pb.n("st"))], inc=("s", 1))
        pb.Y.dma_start(out=hz3[og, :, tt * W:(tt + 1) * W], in_=zsb[:, 0:W],
                       wait=[("s", pb.n("s"))], inc=("st", 16))
    mm(pb, ("til", Wt["WzT"], 8), 1024, 1024, rhs_loc, Tl, c_z, free_sem="s")
    pb.T.wait("s")
    xnfull = A["xnloc"]
    for hf in range(2):
        tof = hf * Tl
        for kc in range(8):
            pb.Y.dma_start(out=xnfull[:, kc * Tl:(kc + 1) * Tl],
                           in_=hbm["agx_out"][hf * 1024 + kc * 128:
                                              hf * 1024 + kc * 128 + 128, :],
                           wait=[("t", pb.n("t")), ("s", pb.n("s")),
                                 ("cc", pb.n("cc")), ("st", pb.n("st"))],
                           inc=("ld", 16))
        pb.barrier()
        def mk_sp(dst_hbm, bias, tof, soft):
            def c(og, tt, ps, t_need):
                if soft:
                    pb.S.activation(out=pb.tmpa[:, 0:W], in_=ps[:, 0:W],
                                    func=AF.Identity, bias=bias[:, og:og + 1],
                                    wait=[("t", t_need), ("st", pb.n("st"))],
                                    inc=("s", 1))
                    emit_softplus(pb, pb.tmpa[:, 0:W], pb.tmpa[:, 0:W], w=W)
                    pb.Y.dma_start(out=dst_hbm[og * 128:(og + 1) * 128,
                                               tof + tt * W: tof + (tt + 1) * W],
                                   in_=pb.tmpa[:, 0:W], wait=[("v", pb.n("v"))],
                                   inc=("st", 16))
                else:
                    pb.S.activation(out=pb.tmpa[:, 0:W], in_=ps[:, 0:W],
                                    func=AF.Identity, bias=bias[:, og:og + 1],
                                    wait=[("t", t_need), ("st", pb.n("st"))],
                                    inc=("s", 1))
                    pb.Y.dma_start(out=dst_hbm[og * 128:(og + 1) * 128,
                                               tof + tt * W: tof + (tt + 1) * W],
                                   in_=pb.tmpa[:, 0:W], wait=[("s", pb.n("s"))],
                                   inc=("st", 16))
            return c
        rhs_f = lambda kc, tt: xnfull[:, kc * Tl + tt * W: kc * Tl + tt * W + W]
        mm(pb, ("til", Wt["WdtTh"], 8), 1024, 512, rhs_f, Tl,
           mk_sp(hbm["hdt"], vt["bdth"], tof, True), free_sem="v")
        pb.T.wait("v")
        mm(pb, ("til", Wt["WxTh"], 8), 1024, 512, rhs_f, Tl,
           mk_sp(hbm["hxs"], vt["bxh"], tof, False), free_sem="s")
        pb.T.wait("s")
        for (wn, r0) in (("WBT", 0), ("WCT", 16)):
            pb.G.wait("t", pb._wfree[0])
            pb.G.dma_start(out=pb.warena[:, 0:128], in_=Wt[wn][0:128, 0:128],
                           inc=("lw", 16))
            for tt in range(Tl // W):
                for kc in range(8):
                    pb.T.matmul(pb.ps_s1[0:16, 0:W],
                                pb.warena[:, kc * 16:(kc + 1) * 16],
                                rhs_f(kc, tt), start=(kc == 0), stop=(kc == 7),
                                wait=[("lw", pb.n("lw")), ("s", pb.n("s")),
                                      ("st", pb.n("st"))] if kc == 0 else None,
                                inc=("t", 1) if kc == 7 else None)
                pb.S.activation(out=pb.tmpa[0:16, 0:W], in_=pb.ps_s1[0:16, 0:W],
                                func=AF.Identity,
                                bias=vt["bBC"][0:16, (0 if r0 == 0 else 1):
                                               (1 if r0 == 0 else 2)],
                                wait=[("t", pb.n("t"))], inc=("s", 1))
                pb.Y.dma_start(out=hbm["hBC"][r0:r0 + 16,
                                              tof + tt * W: tof + (tt + 1) * W],
                               in_=pb.tmpa[0:16, 0:W], wait=[("s", pb.n("s"))],
                               inc=("st", 16))
            pb._wfree[0] = pb.n("t")
        pb.barrier()
    Atile = pb.sqar[:, 3072:3136]
    pb.G.wait("t", pb._wfree[0])
    pb.G.dma_start(out=Atile, in_=Wt["Alog"][:, 0:64], inc=("lw", 16))
    pb.S.activation(out=Atile, in_=Atile, func=AF.Exp,
                    wait=[("lw", pb.n("lw"))], inc=("s", 1))
    pb.V.tensor_scalar_mul(Atile, Atile, -1.0, wait=[("s", pb.n("s"))],
                           inc=("v", 1))
    dtb, xdb, ab, ub, hb, yb = (A["sc1"], A["sc2"], A["sc3"], A["sc4"],
                                A["sc5"], A["sc6"])
    carry = pb.sqar[:, 3136:3152]
    TH = 1024
    hy3 = hbm["hy"]
    for dc in range(4):
        rows = slice(dc * 128, (dc + 1) * 128)
        for hf in range(4):
            pb.Y.wait("v"); pb.Y.wait("s"); pb.Y.wait("st")
            pb.Y.dma_start(out=dtb[:, 0:TH], in_=hbm["hdt"][rows, hf * TH:(hf + 1) * TH],
                           inc=("ld", 16))
            pb.Y.dma_start(out=xdb[:, 0:TH], in_=hbm["hxs"][rows, hf * TH:(hf + 1) * TH],
                           inc=("ld", 16))
            pb.Y.dma_start(out=pb.sqar[0:32, 0:TH], in_=hbm["hBC"][:, hf * TH:(hf + 1) * TH],
                           inc=("ld", 16))
            pb.V.wait("ld")
            pb.S.wait("ld")
            pb.V.tensor_mul(xdb[:, 0:TH], xdb[:, 0:TH], dtb[:, 0:TH], inc=("v", 1))
            pb.V.wait("v")
            pb.V.memset(yb[:, 0:TH], 0.0, inc=("v", 1))
            for n in range(16):
                pb.S.activation(out=ab[:, 0:TH], in_=dtb[:, 0:TH], func=AF.Exp,
                                scale=Atile[:, dc * 16 + n: dc * 16 + n + 1],
                                wait=[("v", pb.n("v")), ("s", pb.n("s"))],
                                inc=("s", 1))
                for c4 in range(TH // W):
                    pb.T.matmul(pb.ps_s2[:, 0:W], A["hsel1"][0:32, n:n + 1].transpose([1, 0])
                                if False else A["sel32"][0:32, n * 128:(n + 1) * 128],
                                pb.sqar[0:32, c4 * W:(c4 + 1) * W],
                                start=True, stop=True,
                                wait=[("s", pb.n("s")), ("v", pb._l1a_f)],
                                inc=("t", 1))
                    pb.V.tensor_mul(ub[:, c4 * W:(c4 + 1) * W],
                                    xdb[:, c4 * W:(c4 + 1) * W], pb.ps_s2[:, 0:W],
                                    wait=[("t", pb.n("t"))], inc=("v", 1))
                    pb._l1a_f = pb.n("v")
                pb.V.wait("v")
                init = 0.0 if hf == 0 else carry[:, n:n + 1]
                pb.V.tensor_tensor_scan(hb[:, 0:TH], ab[:, 0:TH], ub[:, 0:TH],
                                        init, ALU.mult, ALU.add, inc=("v", 1))
                pb.V.wait("v")
                pb.V.tensor_copy(out=carry[:, n:n + 1], in_=hb[:, TH - 1:TH],
                                 inc=("v", 1))
                for c4 in range(TH // W):
                    pb.T.matmul(pb.ps_s2[:, 0:W],
                                A["sel32"][0:32, (16 + n) * 128:(16 + n + 1) * 128],
                                pb.sqar[0:32, c4 * W:(c4 + 1) * W],
                                start=True, stop=True,
                                wait=[("v", pb._l1a_f)], inc=("t", 1))
                    pb.V.scalar_tensor_tensor(pb.tmpa[:, 0:W], pb.ps_s2[:, 0:W],
                                              0.0, hb[:, c4 * W:(c4 + 1) * W],
                                              ALU.bypass, ALU.mult,
                                              wait=[("t", pb.n("t"))])
                    pb.V.tensor_add(yb[:, c4 * W:(c4 + 1) * W],
                                    yb[:, c4 * W:(c4 + 1) * W], pb.tmpa[:, 0:W],
                                    inc=("v", 1))
                    pb._l1a_f = pb.n("v")
            pb.V.wait("v")
            for c4 in range(TH // W):
                pb.V.tensor_copy(out=A["ybf"][:, c4 * W:(c4 + 1) * W],
                                 in_=yb[:, c4 * W:(c4 + 1) * W], inc=("v", 1))
            pb.Y.wait("v")
            pb.Y.dma_start(out=hy3[rows, hf * TH:(hf + 1) * TH],
                           in_=A["ybf"][:, 0:TH], inc=("st", 16))
    pb.barrier()
    pb.G.wait("st")
    pb.G.collective_compute("AllGather", ALU.bypass, replica_groups=groups,
                            ins=[hbm["hy"].opt()], outs=[hbm["ago_out"].opt()],
                            inc=("cc", 1))
    pb.barrier()
    oful = A["oful"]
    acc = A["acc"]
    for tt in range(Tl // W):
        pb.G.wait("t")
        def mkbr(base, tt=tt):
            def f(eng, pbx):
                for kc in range(8):
                    eng.dma_start(out=oful[:, kc * W:(kc + 1) * W],
                                  in_=hbm["ago_out"][kc * 128:(kc + 1) * 128,
                                                     base + tt * W: base + (tt + 1) * W]
                                  ).then_inc(pbx.sems["lw"], 16)
            return f
        rank_switch(pb, "gpsimd", 2, [mkbr(0), mkbr(Tl)], incs=[("lw", 16 * 8)])
        pb.Y.wait("v")
        for kc in range(8):
            pb.Y.dma_start(out=A["xt8b"][:, kc * W:(kc + 1) * W],
                           in_=hz3[kc, :, tt * W:(tt + 1) * W], inc=("ld", 16))
        pb.V.wait("ld"); pb.V.wait("lw")
        for kc in range(8):
            pb.V.tensor_mul(oful[:, kc * W:(kc + 1) * W],
                            oful[:, kc * W:(kc + 1) * W],
                            A["xt8b"][:, kc * W:(kc + 1) * W],
                            inc=("v", 1) if kc == 7 else None)
        pb.barrier(consumers=("tensor",), producers=["v"])
        rhs_of = lambda kc, _tt: oful[:, kc * W:(kc + 1) * W]
        def c_out(og, _t, ps, t_need):
            pb.S.activation(out=acc[:, og * W:(og + 1) * W], in_=ps[:, 0:W],
                            func=AF.Identity, bias=vt["bout"][:, og:og + 1],
                            wait=[("t", t_need)], inc=("s", 1))
        mm(pb, ("til", Wt["WoutT"], 8), 1024, 1024, rhs_of, W, c_out, free_sem="s")
        pb.T.wait("s")
        pb.barrier(consumers=("tensor", "vector", "scalar"), producers=["s"])
        emit_ln_to(pb, acc, None, add_to=xres, w=W, Tl=W, t0=0, span=W,
                   dTl=Tl, dt0=tt * W)
        pb.barrier()


def emit_l3attn(pb, xres, Wt, vt, cmask, cmq, idm, idb, hbsel, hselc, qaug,
                posrow, hbm, groups, A, Tl=TL, cmq_load=None):
    W = 512
    T4 = 4096
    xn0t = A["xt16f"]
    x3 = hbm["xn0"].rearrange("(c p) t -> c p t", p=128)
    a3 = hbm["agx_in"].rearrange("(c p) t -> c p t", p=128)
    for tt in range(Tl // W):
        emit_ln_to(pb, xres, xn0t, w=W, t0=tt * W, span=W, dTl=W, dt0=0)
        pb.Y.wait("v")
        for c in range(8):
            pb.Y.dma_start(out=x3[c, :, tt * W:(tt + 1) * W],
                           in_=xn0t[:, c * W:(c + 1) * W], inc=("st", 16))
        pb.V.wait("st")
        for c in range(8):
            pb.V.tensor_copy(out=A["xt8b"][:, c * W:(c + 1) * W],
                             in_=xn0t[:, c * W:(c + 1) * W],
                             inc=("v", 1) if c == 7 else None)
        pb.Y.wait("v")
        for c in range(8):
            pb.Y.dma_start(out=a3[c, :, tt * W:(tt + 1) * W],
                           in_=A["xt8b"][:, c * W:(c + 1) * W], inc=("st", 16))
    pb.G.wait("st")
    pb.G.collective_compute("AllGather", ALU.bypass, replica_groups=groups,
                            ins=[hbm["agx_in"].opt()], outs=[hbm["agx_out"].opt()],
                            inc=("cc", 1))
    pb.barrier()
    if cmq_load is not None:
        pb.Y.dma_start(out=cmq[:, :], in_=cmq_load[:, :], inc=("ld", 16))
    xnhalf = A["xnfull"]
    Mq = pb.sqar[0:1, 2560:2562]
    Mk = pb.sqar[0:1, 2562:2564]
    sqv = pb.sqar[:, 2048:2560]
    pb.V.memset(pb.sqar[0:1, 2560:2564], 0.0, inc=("v", 1))
    qsb = pb.tmpb.bitcast(BF16)[:, 0:512]
    def mk_rms(dst_hbm, bias, gaint, tof, q8, Mx, rowstep):
        def c(og, tt, ps, t_need):
            pb.S.activation(out=pb.tmpa[:, 0:W], in_=ps[:, 0:W], func=AF.Identity,
                            bias=bias[:, og:og + 1],
                            wait=[("t", t_need), ("v", pb.n("v"))], inc=("s", 1))
            bound_track(pb, Mx, ps, sqv)
            pb.S.activation(out=pb.spa[:, 0:W], in_=pb.tmpa[:, 0:W],
                            func=AF.Square, wait=[("s", pb.n("s"))], inc=("s", 1))
            pb.T.matmul(pb.ps_s1[0:2, 0:W], hselc[:, 0:2], pb.spa[:, 0:W],
                        start=True, stop=True,
                        wait=[("s", pb.n("s")), ("v", pb.n("v"))], inc=("t", 1))
            pb.V.tensor_scalar(pb.spb[0:2, 0:W], pb.ps_s1[0:2, 0:W],
                               1.0 / 64.0, 1e-8, ALU.mult, ALU.add,
                               wait=[("t", pb.n("t"))], inc=("v", 1))
            pb.S.activation(out=pb.spb[0:2, 0:W], in_=pb.spb[0:2, 0:W],
                            func=AF.Ln, wait=[("v", pb.n("v"))], inc=("s", 1))
            pb.S.activation(out=pb.spb[0:2, 0:W], in_=pb.spb[0:2, 0:W],
                            func=AF.Exp, scale=-0.5, wait=[("s", pb.n("s"))],
                            inc=("s", 1))
            if q8:
                pb.V.tensor_scalar_mul(pb.spb[0:2, 0:W], pb.spb[0:2, 0:W],
                                       0.125, wait=[("s", pb.n("s"))],
                                       inc=("v", 1))
            pb.T.matmul(pb.ps_s2[:, 0:W], hbsel[0:2, :], pb.spb[0:2, 0:W],
                        start=True, stop=True,
                        wait=[("s", pb.n("s")), ("v", pb.n("v"))], inc=("t", 1))
            pb.V.tensor_mul(pb.spa[:, 0:W], pb.tmpa[:, 0:W], pb.ps_s2[:, 0:W],
                            wait=[("t", pb.n("t"))], inc=("v", 1))
            pb.V.wait("v")
            pb.V.tensor_scalar_mul(qsb[:, 0:W], pb.spa[:, 0:W], gaint[:, 0:1],
                                   wait=[("st", pb.n("st"))], inc=("v", 1))
            for half in range(2):
                lh = og * 2 + half
                pb.Y.dma_start(
                    out=dst_hbm[lh * rowstep: lh * rowstep + 64,
                                tof + tt * W: tof + (tt + 1) * W],
                    in_=qsb[half * 64:half * 64 + 64, 0:W],
                    wait=[("v", pb.n("v"))], inc=("st", 16))
        return c
    for hf in range(2):
        tof = hf * Tl
        for kc in range(8):
            pb.Y.dma_start(out=xnhalf[:, kc * Tl:(kc + 1) * Tl],
                           in_=hbm["agx_out"][hf * 1024 + kc * 128:
                                              hf * 1024 + kc * 128 + 128, :],
                           wait=[("t", pb.n("t")), ("s", pb.n("s")),
                                 ("cc", pb.n("cc")), ("st", pb.n("st"))],
                           inc=("ld", 16))
        pb.barrier()
        rhs_half = lambda kc, tt: xnhalf[:, kc * Tl + tt * W: kc * Tl + tt * W + W]
        mm(pb, ("til", Wt["wqTh"], 8), 1024, 512, rhs_half, Tl,
           mk_rms(hbm["hq"], vt["bqh"], vt["qnw"], tof, True, Mq, 68),
           free_sem="v")
        pb.T.wait("v")
        mm(pb, ("til", Wt["wkTh"], 8), 1024, 128, rhs_half, Tl,
           mk_rms(hbm["hk"], vt["bkh"], vt["knw"], tof, False, Mk, 68),
           free_sem="v")
        pb.T.wait("v")
        def c_v(og, tt, ps, t_need):
            pb.S.activation(out=qsb[:, 0:W], in_=ps[:, 0:W], func=AF.Identity,
                            bias=vt["bvh"][:, og:og + 1],
                            wait=[("t", t_need), ("st", pb.n("st"))], inc=("s", 1))
            pb.Y.dma_start(out=hbm["hbv"][0:128, tof + tt * W: tof + (tt + 1) * W],
                           in_=qsb[:, 0:W], wait=[("s", pb.n("s"))],
                           inc=("st", 16))
        mm(pb, ("til", Wt["wvTh"], 8), 1024, 128, rhs_half, Tl, c_v, free_sem="s")
        pb.T.wait("s")
        pb.barrier()
    for lh in range(8):
        for a in range(4):
            pb.G.dma_start(out=hbm["hq"][lh * 68 + 64 + a: lh * 68 + 65 + a, :],
                           in_=qaug[4 * lh + a:4 * lh + a + 1, :], inc=("lw", 16))
    for nk in range(2):
        for a in range(4):
            pb.G.dma_start(out=hbm["hk"][nk * 68 + 64 + a: nk * 68 + 65 + a, :],
                           in_=posrow[a:a + 1, :], inc=("lw", 16))
    bound_finish(pb, Mq, Mk, pb.tensors["negB"] if "negB" in pb.tensors
                 else pb.sbuf("negB", [128, 1], F32))
    negB = pb.tensors["negB"]
    pb.barrier()
    k66, v2 = A["k66"], A["v2"]
    for nk in range(2):
        pb.Y.dma_start(out=k66[0:68, nk * T4:(nk + 1) * T4],
                       in_=hbm["hk"][nk * 68:(nk + 1) * 68, :], inc=("ld", 16))
    gin = A["gbq"]
    pb.Y.dma_start(out=gin[:, 0:T4], in_=hbm["hbv"][0:128, :], inc=("ld", 16))
    pb._tt_free = [0, 0]
    pb.V.wait("ld")
    for c in range(T4 // 128):
        bankt = pb.pst if c % 2 == 0 else pb.pst2
        wt_ = [("ld", pb.n("ld")), ("v", pb.n("v"))] if c < 2 else \
              [("v", pb._tt_free[c % 2])]
        pb.T.transpose(bankt[0:128, 0:128], gin[:, c * 128:(c + 1) * 128],
                       idb[:, :], wait=wt_, inc=("t", 1))
        for half in range(2):
            pb.V.tensor_copy(
                out=v2[:, (half * 32 + c) * 65:(half * 32 + c) * 65 + 64],
                in_=bankt[0:128, half * 64:half * 64 + 64],
                wait=[("t", pb.n("t"))] if half == 0 else None,
                inc=("v", 1) if half == 1 else None)
        pb._tt_free[c % 2] = pb.n("v")
    pb.V.wait("v")
    for nk in range(2):
        for c in range(32):
            pb.V.memset(v2[:, (nk * 32 + c) * 65 + 64:(nk * 32 + c) * 65 + 65],
                        1.0, inc=("v", 1) if (nk == 1 and c == 31) else None)
    pb.barrier()
    rowSv = A["rowS"].bitcast(FP16)
    rowSp = [rowSv[:, 0:4096], rowSv[:, 4096:8192]]
    m8 = pb.sqar[:, 3200:3204].bitcast(FP16)
    t64g = pb.sqar[:, 3208:3216]
    q66t = [pb.tmpa.bitcast(BF16)[0:68, 0:512], pb.tmpb.bitcast(BF16)[0:68, 0:512]]
    t64row = pb.sqar[0:1, 3232:3744]
    t64b = A["t64b"]
    pstg = A["pstg"]
    pm01 = A["pm01"]
    if not hasattr(pb, "_l3"):
        pb._l3 = {"q": [0, 0], "o": 0, "ost": 0, "s": [0, 0], "p": [0, 0],
                  "vd": [pb.n("v"), pb.n("v")]}
    F = pb._l3
    for lh in range(8):
        nk = lh // 4
        for tqb in range(8):
            qb = (lh * 8 + tqb) % 2
            wq = [("t", F["q"][qb])]
            if lh == 0 and tqb < 2:
                wq.append(("v", pb.n("v")))
            pb.Y.dma_start(out=q66t[qb], in_=hbm["hq"][lh * 68: lh * 68 + 68,
                                                       tqb * W:(tqb + 1) * W],
                           wait=wq, inc=("ld", 16))
            ldq = pb.n("ld")
            nck = (tqb + 1) * 4
            nc5 = (nck + 3) // 4
            for ti in range(4):
                tig = tqb * 4 + ti
                wcp = min((tig // 4 + 1) * 512, 4096)
                rowS = rowSp[ti % 2]
                for c5 in range((wcp + 511) // 512):
                    bank = pb.ps[c5 % 2]
                    pb.T.matmul(bank[:, 0:512],
                                q66t[qb][0:68, ti * 128:(ti + 1) * 128],
                                k66[0:68, nk * T4 + c5 * 512: nk * T4 + c5 * 512 + 512],
                                start=True, stop=True,
                                wait=[("ld", ldq),
                                      ("s", max(F["s"][c5 % 2], F["p"][c5 % 2])),
                                      ("v", F["vd"][ti % 2])],
                                inc=("t", 1))
                    if c5 == tig // 4:
                        pb.V.tensor_add(bank[:, 0:512], bank[:, 0:512],
                                        cmq[:, (tig % 4) * 512:(tig % 4 + 1) * 512],
                                        wait=[("t", pb.n("t"))], inc=("v", 1))
                        wS = [("v", pb.n("v"))]
                    else:
                        wS = [("t", pb.n("t"))]
                    pb.S.copy(out=rowS[:, c5 * 512:(c5 + 1) * 512],
                              in_=bank[:, 0:512], wait=wS, inc=("s", 1))
                    F["s"][c5 % 2] = pb.n("s")
                pb.V.wait("s")
                for rd in range(8):
                    pb.V.max(out=m8[0:128, 0:8], in_=rowS[:, 0:wcp], inc=("v", 1))
                    pb.V.wait("v")
                    if rd < 7:
                        pb.V.match_replace(out=rowS[:, 0:wcp],
                                           in_to_replace=m8[0:128, 0:8],
                                           in_values=rowS[:, 0:wcp],
                                           imm_value=-50000.0, inc=("v", 1))
                        pb.V.wait("v")
                pb.V.tensor_copy(out=t64g[:, ti:ti + 1], in_=m8[0:128, 7:8],
                                 inc=("v", 1))
                F["vd"][ti % 2] = pb.n("v")
            pb.V.wait("v")
            for ti in range(4):
                bankt = pb.pst if ti % 2 == 0 else pb.pst2
                pb.T.transpose(bankt.bitcast(F32)[0:1, 0:128],
                               t64g[:, ti:ti + 1], idm[:, :],
                               wait=[("v", pb.n("v")), ("s", pb.n("s"))],
                               inc=("t", 1))
                pb.S.copy(out=t64row[0:1, ti * 128:(ti + 1) * 128],
                          in_=bankt.bitcast(F32)[0:1, 0:128],
                          wait=[("t", pb.n("t"))], inc=("s", 1))
            pb.T.matmul(pb.ps_s2[:, 0:512], pb.ones_r[:, :], t64row[0:1, 0:512],
                        start=True, stop=True,
                        wait=[("s", pb.n("s")), ("v", pb.n("v"))], inc=("t", 1))
            pb.V.tensor_copy(out=t64b[:, 0:512], in_=pb.ps_s2[:, 0:512],
                             wait=[("t", pb.n("t"))], inc=("v", 1))
            F["p"][0] = max(F["p"][0], F["s"][0])
            F["p"][1] = max(F["p"][1], F["s"][1])
            stm_done = {}
            def stmm(tkc, wv=None):
                bk_ = tkc % 2
                w = [("s", F["p"][bk_])]
                if wv is not None:
                    w.append(("v", wv))
                pb.T.matmul(pb.ps[bk_][:, 0:512],
                            k66[0:68, nk * T4 + tkc * 128: nk * T4 + tkc * 128 + 128],
                            q66t[qb][0:68, 0:512], start=True, stop=True,
                            wait=w, inc=("t", 1))
                stm_done[tkc] = pb.n("t")
            stmm(0, pb.n("v"))
            for tkc in range(nck):
                bk = tkc % 2
                if tkc + 1 < nck:
                    stmm(tkc + 1)
                if tkc >= tqb * 4:
                    d = tkc - tqb * 4
                    pb.V.tensor_add(pb.ps[bk][:, 0:512], pb.ps[bk][:, 0:512],
                                    cmask[:, d * 512:(d + 1) * 512],
                                    wait=[("t", stm_done[tkc])], inc=("v", 1))
                    wE = [("v", pb.n("v"))]
                else:
                    wE = [("t", stm_done[tkc])]
                pb.V.tensor_tensor(pm01[:, 0:512], pb.ps[bk][:, 0:512],
                                   t64b[:, 0:512], ALU.is_ge, wait=wE,
                                   inc=("v", 1))
                wexp = [("v", pb.n("v"))]
                if tkc == 0:
                    wexp.append(("st", F["ost"]))
                pb.S.activation(out=pstg[:, 0:512], in_=pb.ps[bk][:, 0:512],
                                func=AF.Exp, bias=negB[:, 0:1],
                                wait=wexp, inc=("s", 1))
                F["p"][bk] = pb.n("s")
                pb.V.tensor_mul(pstg[:, 0:512], pstg[:, 0:512], pm01[:, 0:512],
                                wait=[("s", pb.n("s"))], inc=("v", 1))
                pb.T.matmul(pb.ps[2][0:65, 0:512],
                            v2[:, (nk * 32 + tkc) * 65:(nk * 32 + tkc) * 65 + 65],
                            pstg[:, 0:512], start=(tkc == 0),
                            stop=(tkc == nck - 1),
                            wait=[("v", pb.n("v")), ("v", F["o"])],
                            inc=("t", 1))
            F["q"][qb] = pb.n("t")
            pb.V.reciprocal(pb.strow[:, 0:512], pb.ps[2][64:65, 0:512],
                            wait=[("t", pb.n("t")), ("s", pb.n("s"))], inc=("v", 1))
            pb.T.matmul(pb.ps_s1[0:64, 0:512], pb.ones_r[:, 0:64],
                        pb.strow[:, 0:512], start=True, stop=True,
                        wait=[("v", pb.n("v"))], inc=("t", 1))
            pb.S.copy(out=pb.spa[0:64, :], in_=pb.ps[2][0:64, 0:512],
                      wait=[("t", pb.n("t"))], inc=("s", 1))
            pb.V.tensor_mul(pstg[0:64, 0:512], pb.spa[0:64, :],
                            pb.ps_s1[0:64, 0:512],
                            wait=[("s", pb.n("s")), ("st", F["ost"])], inc=("v", 1))
            F["o"] = pb.n("v")
            pb.Y.dma_start(out=hbm["ago_in"][lh * 64:(lh + 1) * 64,
                                             tqb * W:(tqb + 1) * W],
                           in_=pstg[0:64, 0:512], wait=[("v", pb.n("v"))],
                           inc=("st", 16))
            F["ost"] = pb.n("st")
    pb.barrier()
    pb.G.wait("st")
    pb.G.collective_compute("AllGather", ALU.bypass, replica_groups=groups,
                            ins=[hbm["ago_in"].opt()], outs=[hbm["ago_out"].opt()],
                            inc=("cc", 1))
    pb.barrier()
    oful = A["oful"]
    acc = A["acc"]
    for tt in range(Tl // W):
        pb.G.wait("t")
        def mkbr(base, tt=tt):
            def f(eng, pbx):
                for kc in range(8):
                    eng.dma_start(out=oful[:, kc * W:(kc + 1) * W],
                                  in_=hbm["ago_out"][kc * 128:(kc + 1) * 128,
                                                     base + tt * W: base + (tt + 1) * W]
                                  ).then_inc(pbx.sems["lw"], 16)
            return f
        rank_switch(pb, "gpsimd", 2, [mkbr(0), mkbr(Tl)], incs=[("lw", 16 * 8)])
        pb.barrier(consumers=("tensor",), producers=["lw"])
        rhs_of = lambda kc, _tt: oful[:, kc * W:(kc + 1) * W]
        def c_out(og, _t, ps, t_need):
            pb.S.activation(out=acc[:, og * W:(og + 1) * W], in_=ps[:, 0:W],
                            func=AF.Identity, bias=vt["bo"][:, og:og + 1],
                            wait=[("t", t_need)], inc=("s", 1))
        mm(pb, ("til", Wt["woT"], 8), 1024, 1024, rhs_of, W, c_out, free_sem="s")
        pb.T.wait("s")
        pb.barrier(consumers=("tensor", "vector", "scalar"), producers=["s"])
        emit_ln_to(pb, acc, None, add_to=xres, w=W, Tl=W, t0=0, span=W,
                   dTl=Tl, dt0=tt * W)
        pb.barrier()


import os as _os
import math as _math
import ml_dtypes as _mld
from concourse.bass_utils import run_bass_kernel_spmd

GROUPS = [[0, 1], [2, 3], [4, 5], [6, 7]]
NPAIR = 2


def setup_all(pb):
    C = {}
    C["xT"] = pb.inp("xT", [1024, TL])
    C["yT"] = pb.out("yT", [1024, TL])
    cmaski = pb.inp("c_cmask", [128, 2048])
    idmi = pb.inp("c_idm", [128, 128])
    hbseli = pb.inp("c_hbsel", [2, 128])
    hselci = pb.inp("c_hselc", [128, 2])
    sel8i = pb.inp("c_sel8", [8, 1024])
    setup_common(pb)
    C["xres"] = pb.sbuf("xres", [128, 8 * TL], F32)
    C["cmask"] = pb.sbuf("cmask", [128, 2048], F32)
    C["idm"] = pb.sbuf("idmS", [128, 128], F32)
    C["idb"] = pb.sbuf("idbS", [128, 128], BF16)
    C["hbsel"] = pb.sbuf("hbsel", [2, 128], F32)
    C["hselc"] = pb.sbuf("hselc", [128, 2], F32)
    setup_rank(pb)
    alloc_consts(pb)
    alloc_arena(pb, 82176)
    x3 = C["xT"].rearrange("(c p) t -> c p t", p=128)
    for c in range(8):
        pb.Y.dma_start(out=C["xres"][:, c * TL:(c + 1) * TL], in_=x3[c],
                       inc=("ld", 16))
    for (t, i) in ((C["cmask"], cmaski), (C["idm"], idmi), (C["hbsel"], hbseli),
                   (C["hselc"], hselci)):
        pb.Y.dma_start(out=t[:], in_=i[:, :], inc=("ld", 16))
    pb.G.dma_start(out=C["idb"][:], in_=idmi[:, :], inc=("lw", 16))
    pb.barrier()
    return C


def alloc_consts(pb):
    for n, w in (("ln1g", 8), ("ln1b", 8), ("ln2g", 8), ("ln2b", 8),
                 ("gateb", 4), ("pb_", 4), ("outb", 8), ("m1b", 32), ("m2b", 8),
                 ("clockb", 1)):
        pb.sbuf("a0t_" + n, [128, w], F32)
    pb.sbuf("a0_perm", [128, 128], BF16)
    for n, w in (("s1", 32), ("s2", 32), ("sh1", 32), ("sh2", 32), ("s3", 8)):
        pb.sbuf("f0t_" + n, [128, w], F32)
    for n, w in (("bz", 8), ("bxh", 4), ("bdth", 4), ("bBC", 2), ("bout", 8)):
        pb.sbuf("a1t_" + n, [128, w], F32)
    pb.sbuf("f1_g1t", [128, 8], F32)
    pb.sbuf("f1_g2t", [128, 8], F32)
    for n in ("bq", "bk", "bv", "bo"):
        pb.sbuf("a2t_" + n, [128, 8], F32)
    pb.sbuf("f2_rb", [128, 1], F32)
    for n, w in (("bqh", 4), ("bkh", 1), ("bvh", 1), ("bo", 8), ("qnw", 1),
                 ("knw", 1)):
        pb.sbuf("a3t_" + n, [128, w], F32)


_aprobe = [0]


def alloc_arena(pb, need):
    lo, hi = 0, 130000
    while lo < hi:
        mid = (lo + hi + 1) // 2
        _aprobe[0] += 1
        try:
            with pb.nc.sbuf_tensor(f"_ap{_aprobe[0]}", [128, mid], U8):
                pass
            lo = mid
        except AssertionError:
            hi = mid - 1
    lo -= 64
    assert lo >= need, f"arena {lo} < {need}"
    pb.arena = pb.sbuf("bigarena", [128, lo], U8)


def finish_all(pb, C):
    y3 = C["yT"].rearrange("(c p) t -> c p t", p=128)
    for c in range(8):
        pb.Y.dma_start(out=y3[c], in_=C["xres"][:, c * TL:(c + 1) * TL],
                       inc=("st", 16))
    pb.Y.wait("st")


def bb_a0(pb, C):
    names = {"WqTh": [4 * 128, 8 * 128], "WkTh": [4 * 128, 8 * 128],
             "WvTh": [4 * 128, 8 * 128], "gateTh": [4 * 128, 8 * 128],
             "pTh": [4 * 128, 8 * 128], "clockTh": [128, 64],
             "out_wT": [8 * 128, 8 * 128], "mlp1T": [32 * 128, 8 * 128],
             "mlp2T": [8 * 128, 32 * 128]}
    Wt = {n: pb.inp("a0_" + n, s) for n, s in names.items()}
    vnames = {"ln1g": 8, "ln1b": 8, "ln2g": 8, "ln2b": 8, "gateb": 4,
              "pb_": 4, "outb": 8, "m1b": 32, "m2b": 8}
    vin = {n: pb.inp("a0_vb_" + n, [128, w]) for n, w in vnames.items()}
    clockbi = pb.inp("a0_clockbi", [128, 1])
    permi = pb.inp("a0_permi", [128, 128])
    ropeCd = pb.inp("a0_ropeCd", [128, 4096], BF16)
    ropeSd = pb.inp("a0_ropeSd", [128, 4096], BF16)
    vt = {n: pb.tensors["a0t_" + n] for n in vnames}
    vt["clockb"] = pb.tensors["a0t_clockb"]
    perm = pb.tensors["a0_perm"]
    A = {
        "xnfull": view(pb, 0, 8 * 2048, BF16),
        "gp1": view(pb, 0, 4096, F32), "gp2": view(pb, 16384, 4096, F32),
        "gp3": view(pb, 32768, 4096, F32), "gp4": view(pb, 49152, 4096, F32),
        "gbq": view(pb, 65536, 4096, BF16), "gbk": view(pb, 73728, 4096, BF16),
        "k4": view(pb, 0, 4 * 4096, BF16),
        "v2": view(pb, 32768, 4 * 32 * 65, BF16),
        "oful": view(pb, 0, 8 * 512, BF16),
        "xt16f": view(pb, 8192, 8 * 512, F32),
        "xt8b": view(pb, 24576, 8 * 512, BF16),
        "hid32": view(pb, 32768, 32 * 512, BF16),
        "acc": view(pb, 65536, 8 * 512, F32),
    }
    hbm = {
        "xn0": pb.dram("a0_xn0", [1024, TL], F32),
        "agx_in": pb.dram("a0_agx_in", [1024, TL], BF16),
        "agx_out": pb.dram("a0_agx_out", [NPAIR * 1024, TL], BF16),
        "hbq": pb.dram("a0_hbq", [512, 4096], BF16),
        "hbk": pb.dram("a0_hbk", [512, 4096], BF16),
        "hbv": pb.dram("a0_hbv", [512, 4096], BF16),
        "hbg": pb.dram("a0_hbg", [512, 4096], BF16),
        "hbp": pb.dram("a0_hbp", [512, 4096], BF16),
        "hclk": pb.dram("a0_hclk", [8, 4096], F32),
        "hq": pb.dram("a0_hq", [512, 4096], BF16),
        "hk": pb.dram("a0_hk", [512, 4096], BF16),
        "ago_in": pb.dram("a0_ago_in", [512, 4096], BF16),
        "ago_out": pb.dram("a0_ago_out", [NPAIR * 512, 4096], BF16),
    }
    for n in vnames:
        pb.Y.dma_start(out=vt[n][:], in_=vin[n][:, :], inc=("ld", 16))
    pb.Y.dma_start(out=vt["clockb"][:], in_=clockbi[:, :], inc=("ld", 16))
    pb.G.dma_start(out=perm[:], in_=permi[:, :], inc=("lw", 16))
    pb.barrier()
    emit_l0attn(pb, C["xres"], Wt, vt, C["cmask"], C["idm"], C["idb"],
                C["hbsel"], C["hselc"], perm, ropeCd, ropeSd, hbm, GROUPS, A)
    pb.barrier()


def bb_f0(pb, C):
    w1T = pb.inp("f0_w1T", [32 * 128, 8 * 128])
    w2T = pb.inp("f0_w2T", [32 * 128, 8 * 128])
    h1T = pb.inp("f0_h1T", [32 * 128, 32 * 128])
    h2T = pb.inp("f0_h2T", [32 * 128, 32 * 128])
    w3T = pb.inp("f0_w3T", [8 * 128, 32 * 128])
    sin_ = {}
    for nm, wd in (("s1", 32), ("s2", 32), ("sh1", 32), ("sh2", 32), ("s3", 8)):
        sin_[nm] = pb.inp("f0_" + nm, [128, wd])
    st = {nm: pb.tensors["f0t_" + nm]
          for nm in ("s1", "s2", "sh1", "sh2", "s3")}
    xnw = view(pb, 0, 8 * 512, BF16)
    hbufA = view(pb, 8192, 32 * 512, BF16)
    hbufB = view(pb, 40960, 32 * 512, BF16)
    acc = view(pb, 73728, 8 * 512, BF16)
    for nm in st:
        pb.Y.dma_start(out=st[nm][:], in_=sin_[nm][:, :], inc=("ld", 16))
    pb.barrier()
    emit_l0ffn(pb, C["xres"], acc, w1T, w2T, h1T, h2T, w3T,
               st["s1"], st["s2"], st["sh1"], st["sh2"], st["s3"],
               hbufA, hbufB, xnw)
    pb.barrier()


def bb_a1(pb, C):
    Wt = {"WzT": pb.inp("a1_WzT", [8 * 128, 8 * 128]),
          "WxTh": pb.inp("a1_WxTh", [4 * 128, 8 * 128]),
          "WdtTh": pb.inp("a1_WdtTh", [4 * 128, 8 * 128]),
          "WBT": pb.inp("a1_WBT", [128, 128]),
          "WCT": pb.inp("a1_WCT", [128, 128]),
          "WoutT": pb.inp("a1_WoutT", [8 * 128, 8 * 128]),
          "Alog": pb.inp("a1_Alog", [128, 64])}
    vtin = {"bz": pb.inp("a1_v_bz", [128, 8]), "bxh": pb.inp("a1_v_bxh", [128, 4]),
            "bdth": pb.inp("a1_v_bdth", [128, 4]),
            "bBC": pb.inp("a1_v_bBC", [128, 2]),
            "bout": pb.inp("a1_v_bout", [128, 8])}
    sel32i = pb.inp("a1_sel32i", [32, 32 * 128])
    vts = {}
    for n, ap in vtin.items():
        t = pb.tensors["a1t_" + n]
        pb.Y.dma_start(out=t[:], in_=ap[:, :], inc=("ld", 16))
        vts[n] = t
    sel32 = view(pb, 65536, 4096, F32)[0:32, 0:4096]
    pb.barrier(consumers=("sync",))
    pb.Y.dma_start(out=sel32[:, :], in_=sel32i[:, :], inc=("ld", 16))
    A = {
        "xnloc": view(pb, 0, 8 * 2048, BF16),
        "sc1": view(pb, 32768, 1024, F32), "sc2": view(pb, 36864, 1024, F32),
        "sc3": view(pb, 40960, 1024, F32), "sc4": view(pb, 45056, 1024, F32),
        "sc5": view(pb, 49152, 1024, F32), "sc6": view(pb, 53248, 1024, F32),
        "ybf": view(pb, 57344, 1024, BF16),
        "oful": view(pb, 0, 8 * 512, BF16),
        "xt16f": view(pb, 8192, 8 * 512, F32),
        "xt8b": view(pb, 24576, 8 * 512, BF16),
        "acc": view(pb, 49152, 8 * 512, F32),
        "sel32": sel32,
    }
    hbm = {
        "xn0": pb.dram("a1_xn0", [1024, TL], F32),
        "agx_in": pb.dram("a1_agx_in", [1024, TL], BF16),
        "agx_out": pb.dram("a1_agx_out", [NPAIR * 1024, TL], BF16),
        "hz": pb.dram("a1_hz", [1024, TL], BF16),
        "hdt": pb.dram("a1_hdt", [512, 4096], F32),
        "hxs": pb.dram("a1_hxs", [512, 4096], F32),
        "hBC": pb.dram("a1_hBC", [32, 4096], F32),
        "hy": pb.dram("a1_hy", [512, 4096], BF16),
        "ago_out": pb.dram("a1_ago_out", [NPAIR * 512, 4096], BF16),
    }
    pb.barrier()
    emit_l1attn(pb, C["xres"], Wt, vts, hbm, GROUPS, A)
    pb.barrier()


def bb_f1(pb, C):
    g1i = pb.inp("f1_g1i", [128, 8]); g2i = pb.inp("f1_g2i", [128, 8])
    WdkvT = pb.inp("f1_WdkvT", [4 * 128, 8 * 128])
    WqTh = pb.inp("f1_WqTh", [4 * 128, 8 * 128])
    WukTh = [pb.inp(f"f1_WukTh{g}", [2 * 128, 4 * 128]) for g in range(2)]
    WuvTh = [pb.inp(f"f1_WuvTh{g}", [2 * 128, 4 * 128]) for g in range(2)]
    WoT = pb.inp("f1_WoT", [8 * 128, 8 * 128])
    gatewT = pb.inp("f1_gatewT", [128, 64])
    w1T = [pb.inp(f"f1_w1T{e}", [16 * 128, 8 * 128]) for e in range(8)]
    w2T = [pb.inp(f"f1_w2T{e}", [8 * 128, 16 * 128]) for e in range(8)]
    g1t = pb.tensors["f1_g1t"]
    g2t = pb.tensors["f1_g2t"]
    A = {
        "xnfull": view(pb, 0, 8 * 2048, BF16),
        "cbuf": view(pb, 32768, 4 * 4096, BF16),
        "k4": view(pb, 0, 4 * 4096, BF16),
        "v2": view(pb, 65536, 4 * 32 * 65, BF16),
        "xt16f": view(pb, 0, 8 * 512, F32),
        "xt8b": view(pb, 16384, 8 * 512, BF16),
        "hid": view(pb, 24576, 16 * 512, BF16),
        "acc": view(pb, 40960, 8 * 512, F32),
        "oful": view(pb, 57344, 8 * 512, BF16),
    }
    hbm = {
        "xn0": pb.dram("f1_xn0", [1024, TL], F32),
        "hq": pb.dram("f1_hq", [512, 4096], BF16),
        "hk": pb.dram("f1_hk", [512, 4096], BF16),
        "agx_in": pb.dram("f1_agx_in", [1024, TL], BF16),
        "agx_out": pb.dram("f1_agx_out", [NPAIR * 1024, TL], BF16),
        "ago_in": pb.dram("f1_ago_in", [512, 4096], BF16),
        "ago_out": pb.dram("f1_ago_out", [NPAIR * 512, 4096], BF16),
    }
    sel8v = view(pb, 65536, 1024, F32)[0:8, 0:1024]
    pb.Y.dma_start(out=g1t[:], in_=g1i[:, :], inc=("ld", 16))
    pb.Y.dma_start(out=g2t[:], in_=g2i[:, :], inc=("ld", 16))
    pb.barrier()
    emit_l1ffn(pb, C["xres"], g1t, g2t, WdkvT, WqTh, WukTh, WuvTh, WoT, gatewT,
               w1T, w2T, sel8v, C["cmask"], C["idm"], C["idb"], hbm,
               GROUPS, A, sel8_load=pb.inputs["c_sel8"])
    pb.barrier()


def bb_a2(pb, C):
    Ws = {}
    for nm in ("WqT", "WkT", "WvT", "WoT"):
        Ws[nm] = pb.inp("a2_" + nm, [8 * 128, 8 * 128])
    for nm in ("bq", "bk", "bv", "bo"):
        Ws[nm] = pb.inp("a2_" + nm, [128, 8])
    xn = view(pb, 0, 8 * TL, BF16)
    fkbuf = view(pb, 32768, 2 * 16 * 64, BF16)
    v2buf = view(pb, 36864, 2 * 16 * 65, BF16)
    qbuf = view(pb, 32768, 8 * TL, BF16)
    outbuf = view(pb, 0, 8 * TL, BF16)
    accv = view(pb, 32768, 8 * 512, F32)
    bt = {nm: pb.tensors["a2t_" + nm] for nm in ("bq", "bk", "bv", "bo")}
    cc_in = pb.dram("a2_cc_in", [64, 16 * 65], F32)
    cc_out = pb.dram("a2_cc_out", [64, 16 * 65], F32)
    for nm in bt:
        pb.Y.dma_start(out=bt[nm][:], in_=Ws[nm][:, :], inc=("ld", 16))
    pb.barrier()
    emit_l2attn(pb, C["xres"], Ws["WqT"], Ws["WkT"], Ws["WvT"], Ws["WoT"],
                bt["bq"], bt["bk"], bt["bv"], bt["bo"], C["hbsel"], C["idb"],
                xn, fkbuf, v2buf, qbuf, outbuf, accv, cc_in, cc_out, GROUPS)
    pb.barrier()


def bb_f2(pb, C):
    sgT = pb.inp("f2_sgT", [16 * 128, 8 * 128])
    suT = pb.inp("f2_suT", [16 * 128, 8 * 128])
    sdT = pb.inp("f2_sdT", [8 * 128, 16 * 128])
    routerT = pb.inp("f2_routerT", [128, 64])
    rbv = pb.inp("f2_rbv", [128, 1])
    gwT = [pb.inp(f"f2_gwT{e}", [16 * 128, 8 * 128]) for e in range(8)]
    uwT = [pb.inp(f"f2_uwT{e}", [16 * 128, 8 * 128]) for e in range(8)]
    dwT = [pb.inp(f"f2_dwT{e}", [8 * 128, 16 * 128]) for e in range(8)]
    xnw = view(pb, 0, 8 * 512, BF16)
    gbuf = view(pb, 8192, 16 * 512, BF16)
    acc = view(pb, 8192 + 16384, 8 * 512, F32)
    rb = pb.tensors["f2_rb"]
    sel8v = view(pb, 40960, 1024, F32)[0:8, 0:1024]
    pb.Y.dma_start(out=rb[:], in_=rbv[:, :], inc=("ld", 16))
    pb.barrier(consumers=("sync",))
    pb.Y.dma_start(out=sel8v[:, :], in_=pb.inputs["c_sel8"][:, :], inc=("ld", 16))
    pb.barrier()
    emit_l2ffn(pb, C["xres"], sgT, suT, sdT, routerT, rb, gwT, uwT, dwT,
               sel8v, C["idm"], xnw, gbuf, None, acc)
    pb.barrier()


def bb_a3(pb, C):
    Wt = {"wqTh": pb.inp("a3_wqTh", [4 * 128, 8 * 128]),
          "wkTh": pb.inp("a3_wkTh", [128, 8 * 128]),
          "wvTh": pb.inp("a3_wvTh", [128, 8 * 128]),
          "woT": pb.inp("a3_woT", [8 * 128, 8 * 128])}
    vt_in = {"bqh": pb.inp("a3_v_bqh", [128, 4]),
             "bkh": pb.inp("a3_v_bkh", [128, 1]),
             "bvh": pb.inp("a3_v_bvh", [128, 1]),
             "bo": pb.inp("a3_v_bo", [128, 8]),
             "qnw": pb.inp("a3_v_qnw", [128, 1]),
             "knw": pb.inp("a3_v_knw", [128, 1])}
    cmqi = pb.inp("a3_cmqi", [128, 4 * 512])
    qaug = pb.inp("a3_qaug", [32, 4096])
    posrow = pb.inp("a3_posrow", [4, 4096])
    vt = {}
    for n, ap in vt_in.items():
        t = pb.tensors["a3t_" + n]
        pb.Y.dma_start(out=t[:], in_=ap[:, :], inc=("ld", 16))
        vt[n] = t
    cmq = view(pb, 61440, 2048, F32)
    A = {
        "xnfull": view(pb, 0, 8 * 2048, BF16),
        "k66": view(pb, 0, 2 * 4096, BF16),
        "v2": view(pb, 16384, 2 * 32 * 65, BF16),
        "gbq": view(pb, 24832, 4096, BF16),
        "rowS": view(pb, 32768, 4096, F32),
        "t64b": view(pb, 49152, 512, F32),
        "pstg": view(pb, 51200, 512, BF16),
        "pm01": view(pb, 52224, 512, BF16),
        "oful": view(pb, 53248, 8 * 512, BF16),
        "xt16f": view(pb, 61440, 8 * 512, F32),
        "xt8b": view(pb, 53248, 8 * 512, BF16),
        "acc": view(pb, 32768, 8 * 512, F32),
    }
    hbm = {
        "xn0": pb.dram("a3_xn0", [1024, TL], F32),
        "agx_in": pb.dram("a3_agx_in", [1024, TL], BF16),
        "agx_out": pb.dram("a3_agx_out", [NPAIR * 1024, TL], BF16),
        "hq": pb.dram("a3_hq", [544, 4096], BF16),
        "hk": pb.dram("a3_hk", [136, 4096], BF16),
        "hbv": pb.dram("a3_hbv", [128, 4096], BF16),
        "ago_in": pb.dram("a3_ago_in", [512, 4096], BF16),
        "ago_out": pb.dram("a3_ago_out", [NPAIR * 512, 4096], BF16),
    }
    pb.barrier()
    emit_l3attn(pb, C["xres"], Wt, vt, C["cmask"], cmq, C["idm"], C["idb"],
                C["hbsel"], C["hselc"], qaug, posrow, hbm, GROUPS, A,
                cmq_load=cmqi)
    pb.barrier()


def bb_f3(pb, C):
    th = pb.inp("f3_th", [1024, 1024])
    tn = pb.inp("f3_tn", [1024, 1024])
    ph = pb.inp("f3_ph", [1024, 1024])
    pn = pb.inp("f3_pn", [1024, 1024])
    cosw = pb.dram("f3_cosw", [1024, 1024], BF16)
    sinw = pb.dram("f3_sinw", [1024, 1024], BF16)
    xn = view(pb, 0, 8 * TL, BF16)
    acc = view(pb, 32768, 8 * TL, BF16)
    pb.barrier()
    emit_l3ffn_weights_prep(pb, th, tn, ph, pn, cosw, sinw)
    emit_ln_to(pb, C["xres"], xn)
    pb.barrier()
    emit_l3ffn(pb, xn, acc, cosw, sinw)
    emit_ln_to(pb, acc, None, add_to=C["xres"], bf_in=True)
    pb.barrier()


BLOCK_BUILDERS = [bb_a0, bb_f0, bb_a1, bb_f1, bb_a2, bb_f2, bb_a3, bb_f3]


def build_program(block_ids):
    pb = PB()
    C = setup_all(pb)
    for bi in block_ids:
        BLOCK_BUILDERS[bi](pb, C)
    finish_all(pb, C)
    return pb, pb.finish()


def _bfr(x):
    x32 = np.ascontiguousarray(np.asarray(x, np.float32))
    u = x32.view(np.uint32)
    u2 = ((u + 0x7FFF + ((u >> 16) & 1)) & np.uint32(0xFFFF0000)).astype(np.uint32)
    return u2.view(np.float32)


def prep_maps(P):
    f32 = np.float32
    cm = np.zeros((128, 2048), f32)
    for d in range(4):
        i = np.arange(128)[:, None]; j = np.arange(512)[None, :]
        cm[:, d * 512:(d + 1) * 512] = np.where(i + 128 * d > j, -1e30, 0.0)
    hb = np.zeros((2, 128), f32); hb[0, 0:64] = 1; hb[1, 64:128] = 1
    hc = np.zeros((128, 2), f32); hc[0:64, 0] = 1; hc[64:128, 1] = 1
    sel8 = np.zeros((8, 1024), f32)
    for e in range(8):
        sel8[e, e * 128:(e + 1) * 128] = 1.0
    base = {"c_cmask": cm, "c_idm": np.eye(128, dtype=f32), "c_hbsel": hb,
            "c_hselc": hc, "c_sel8": sel8}

    def ct(a):
        return np.ascontiguousarray(np.asarray(a, f32))

    pm = np.zeros((128, 128), f32)
    for k in range(128):
        pm[k, k ^ 1] = 1.0
    half = 32
    freqs = np.exp(-np.arange(half) * (np.log(10000.0) / half))
    tpos = np.arange(4096)
    thr = tpos[None, :] * freqs.repeat(2)[:, None]
    Cr = np.cos(thr); Sr = np.sin(thr)
    Ss = Sr.copy(); Ss[0::2, :] *= -1.0
    a0_shared = {
        "a0_out_wT": tile_lhsT(P["L0_attn_out_w"]),
        "a0_mlp1T": tile_lhsT(P["L0_attn_mlp_w1"]),
        "a0_mlp2T": tile_lhsT(P["L0_attn_mlp_w2"]),
        "a0_vb_ln1g": pvec(P["L0_attn_ln1_g"]), "a0_vb_ln1b": pvec(P["L0_attn_ln1_b"]),
        "a0_vb_ln2g": pvec(P["L0_attn_ln2_g"]), "a0_vb_ln2b": pvec(P["L0_attn_ln2_b"]),
        "a0_vb_outb": pvec(P["L0_attn_out_b"]), "a0_vb_m1b": pvec(P["L0_attn_mlp_b1"]),
        "a0_vb_m2b": pvec(P["L0_attn_mlp_b2"]),
        "a0_permi": pm,
        "a0_ropeCd": np.tile(Cr, (2, 1)).astype(_mld.bfloat16),
        "a0_ropeSd": np.tile(Ss, (2, 1)).astype(_mld.bfloat16),
    }
    f0 = {"f0_w1T": tile_lhsT(ct(P["L0_ffn_w1_w"]).T),
          "f0_w2T": tile_lhsT(ct(P["L0_ffn_w2_w"]).T),
          "f0_h1T": tile_lhsT(ct(P["L0_ffn_h1_w"]).T),
          "f0_h2T": tile_lhsT(ct(P["L0_ffn_h2_w"]).T),
          "f0_w3T": tile_lhsT(ct(P["L0_ffn_w3_w"]).T),
          "f0_s1": ct(P["L0_ffn_w1_s"]).reshape(32, 128).T,
          "f0_s2": ct(P["L0_ffn_w2_s"]).reshape(32, 128).T,
          "f0_sh1": ct(P["L0_ffn_h1_s"]).reshape(32, 128).T,
          "f0_sh2": ct(P["L0_ffn_h2_s"]).reshape(32, 128).T,
          "f0_s3": ct(P["L0_ffn_w3_s"]).reshape(8, 128).T}
    f0 = {k: np.ascontiguousarray(v) for k, v in f0.items()}
    sel32 = np.zeros((32, 32 * 128), f32)
    for n in range(32):
        sel32[n, n * 128:(n + 1) * 128] = 1.0
    bBC = np.zeros((128, 2), f32)
    bBC[0:16, 0] = ct(P["L1_attn_bB"]); bBC[0:16, 1] = ct(P["L1_attn_bC"])
    a1_shared = {
        "a1_WzT": tile_lhsT(P["L1_attn_Wz"]),
        "a1_WBT": np.ascontiguousarray(
            ct(P["L1_attn_WB"]).reshape(8, 128, 16).transpose(1, 0, 2).reshape(128, 128)),
        "a1_WCT": np.ascontiguousarray(
            ct(P["L1_attn_WC"]).reshape(8, 128, 16).transpose(1, 0, 2).reshape(128, 128)),
        "a1_WoutT": tile_lhsT(P["L1_attn_Wout"]),
        "a1_v_bz": pvec(P["L1_attn_bz"]), "a1_v_bBC": bBC,
        "a1_v_bout": pvec(P["L1_attn_bout"]), "a1_sel32i": sel32,
    }
    gw = ct(P["L1_ffn_gate_w"])
    f1_shared = {
        "f1_g1i": pvec(P["L1_ffn_g1"]), "f1_g2i": pvec(P["L1_ffn_g2"]),
        "f1_WdkvT": tile_lhsT(P["L1_ffn_Wdkv"]),
        "f1_WoT": tile_lhsT(P["L1_ffn_Wo"]),
        "f1_gatewT": np.ascontiguousarray(
            gw.reshape(8, 128, 8).transpose(1, 0, 2).reshape(128, 64)),
    }
    for e in range(8):
        f1_shared[f"f1_w1T{e}"] = tile_lhsT(ct(P["L1_ffn_w1"])[e])
        f1_shared[f"f1_w2T{e}"] = tile_lhsT(ct(P["L1_ffn_w2"])[e])
    a2 = {"a2_WqT": tile_lhsT(P["L2_attn_Wq"]), "a2_WkT": tile_lhsT(P["L2_attn_Wk"]),
          "a2_WvT": tile_lhsT(P["L2_attn_Wv"]), "a2_WoT": tile_lhsT(P["L2_attn_Wo"]),
          "a2_bq": pvec(P["L2_attn_bq"]), "a2_bk": pvec(P["L2_attn_bk"]),
          "a2_bv": pvec(P["L2_attn_bv"]), "a2_bo": pvec(P["L2_attn_bo"])}
    rb128 = np.zeros((128, 1), f32)
    rb128[:8, 0] = ct(P["L2_ffn_routing_bias"])
    f2 = {"f2_sgT": tile_lhsT(ct(P["L2_ffn_shared_gate"]).T),
          "f2_suT": tile_lhsT(ct(P["L2_ffn_shared_up"]).T),
          "f2_sdT": tile_lhsT(ct(P["L2_ffn_shared_down"]).T),
          "f2_routerT": np.ascontiguousarray(
              ct(P["L2_ffn_router_w"]).T.reshape(8, 128, 8)
              .transpose(1, 0, 2).reshape(128, 64)),
          "f2_rbv": rb128}
    for e in range(8):
        f2[f"f2_gwT{e}"] = tile_lhsT(ct(P["L2_ffn_gate_w"])[e])
        f2[f"f2_uwT{e}"] = tile_lhsT(ct(P["L2_ffn_up_w"])[e])
        f2[f"f2_dwT{e}"] = tile_lhsT(ct(P["L2_ffn_down_w"])[e])
    slopes = (2.0 ** (-(np.arange(16) / 15.0 * 8.0))).astype(f32)
    pos = np.arange(4096, dtype=f32)
    phh = np.float32(32.0) * np.float32(pos // 32)
    pll = (pos - phh).astype(f32)
    posrow = np.stack([phh, pll, np.ones(4096, f32), np.ones(4096, f32)])
    cmq = np.zeros((128, 2048), f32)
    for dd in range(4):
        i = np.arange(128)[:, None]; j = np.arange(512)[None, :]
        cmq[:, dd * 512:(dd + 1) * 512] = np.where(j > i + 128 * dd, -50000.0,
                                                   0.0)
    a3_shared = {
        "a3_woT": tile_lhsT(P["L3_attn_wo"]),
        "a3_v_bo": pvec(P["L3_attn_bo"]),
        "a3_v_qnw": np.ascontiguousarray(np.tile(ct(P["L3_attn_qn_w"]), 2)[:, None]),
        "a3_v_knw": np.ascontiguousarray(np.tile(ct(P["L3_attn_kn_w"]), 2)[:, None]),
        "a3_cmqi": cmq, "a3_posrow": posrow,
    }
    f3 = {"f3_th": ct(P["L3_ffn_theta"]), "f3_tn": ct(P["L3_ffn_theta_noise"]),
          "f3_ph": ct(P["L3_ffn_phi"]), "f3_pn": ct(P["L3_ffn_phi_noise"])}

    mapr = []
    for r in range(2):
        cs = slice(r * 512, (r + 1) * 512)
        m = dict(base)
        m.update(a0_shared); m.update(f0); m.update(a1_shared)
        m.update(f1_shared); m.update(a2); m.update(f2)
        m.update(a3_shared); m.update(f3)
        m["a0_WqTh"] = tile_lhsT(ct(P["L0_attn_Wq"])[:, cs])
        m["a0_WkTh"] = tile_lhsT(ct(P["L0_attn_Wk"])[:, cs])
        m["a0_WvTh"] = tile_lhsT(ct(P["L0_attn_Wv"])[:, cs])
        m["a0_gateTh"] = tile_lhsT(ct(P["L0_attn_gate_w"])[:, cs])
        m["a0_pTh"] = tile_lhsT(ct(P["L0_attn_p_w"])[:, cs])
        m["a0_clockTh"] = np.ascontiguousarray(
            ct(P["L0_attn_clock_w"])[:, r * 8:(r + 1) * 8].reshape(8, 128, 8)
            .transpose(1, 0, 2).reshape(128, 64))
        m["a0_vb_gateb"] = pvec(ct(P["L0_attn_gate_b"])[cs])
        m["a0_vb_pb_"] = pvec(ct(P["L0_attn_p_b"])[cs])
        cb = np.zeros((128, 1), f32)
        cb[0:8, 0] = ct(P["L0_attn_clock_b"])[r * 8:(r + 1) * 8]
        m["a0_clockbi"] = cb
        m["a1_WxTh"] = tile_lhsT(ct(P["L1_attn_Wx"])[:, cs])
        m["a1_WdtTh"] = tile_lhsT(ct(P["L1_attn_Wdt"])[:, cs])
        m["a1_Alog"] = np.ascontiguousarray(
            ct(P["L1_attn_A_log"])[cs].reshape(4, 128, 16).transpose(1, 0, 2)
            .reshape(128, 64))
        m["a1_v_bxh"] = pvec(ct(P["L1_attn_bx"])[cs])
        m["a1_v_bdth"] = pvec(ct(P["L1_attn_bdt"])[cs])
        m["f1_WqTh"] = tile_lhsT(ct(P["L1_ffn_Wq"])[:, cs])
        for g in range(2):
            c0 = (r * 8 + g * 4) * 64
            m[f"f1_WukTh{g}"] = tile_lhsT(ct(P["L1_ffn_Wuk"])[:, c0:c0 + 256])
            m[f"f1_WuvTh{g}"] = tile_lhsT(ct(P["L1_ffn_Wuv"])[:, c0:c0 + 256])
        kcols = slice(2 * r * 64, 2 * r * 64 + 128)
        vcols = slice(256 + 2 * r * 64, 256 + 2 * r * 64 + 128)
        m["a3_wqTh"] = tile_lhsT(ct(P["L3_attn_wq"])[:, r * 512:(r + 1) * 512])
        m["a3_wkTh"] = tile_lhsT(ct(P["L3_attn_wkv"])[:, kcols])
        m["a3_wvTh"] = tile_lhsT(ct(P["L3_attn_wkv"])[:, vcols])
        m["a3_v_bqh"] = pvec(ct(P["L3_attn_bq"])[r * 512:(r + 1) * 512])
        m["a3_v_bkh"] = pvec(ct(P["L3_attn_bkv"])[kcols])
        m["a3_v_bvh"] = pvec(ct(P["L3_attn_bkv"])[vcols])
        qa = np.zeros((32, 4096), f32)
        for lh in range(8):
            sgb = _bfr(slopes[r * 8 + lh])
            t = (-np.float64(sgb) * pos.astype(np.float64)).astype(f32)
            qh = _bfr(t)
            qa[4 * lh, :] = sgb
            qa[4 * lh + 1, :] = sgb
            qa[4 * lh + 2, :] = qh
            qa[4 * lh + 3, :] = t - qh
        m["a3_qaug"] = qa
        mapr.append(m)
    return mapr


def tile_lhsT(Wko):
    Wko = np.ascontiguousarray(np.asarray(Wko, np.float32))
    K, O = Wko.shape
    kt, nog = K // 128, O // 128
    return np.ascontiguousarray(
        Wko.reshape(kt, 128, nog, 128).transpose(2, 1, 0, 3).reshape(nog * 128,
                                                                     kt * 128))


def pvec(v):
    v = np.ascontiguousarray(np.asarray(v, np.float32))
    n = v.shape[0] // 128
    return np.ascontiguousarray(v.reshape(n, 128).T)


_PROGS = {}


def _get_prog(key, block_ids):
    if key not in _PROGS:
        _PROGS[key] = build_program(block_ids)
    return _PROGS[key]


FUSED = not bool(_os.environ.get("MK_UNFUSED"))


def kernel(**inputs):
    x = np.asarray(inputs["L0_attn_x"], np.float32)
    mapr = prep_maps(inputs)
    shards = []
    for c in range(8):
        b, r = c // 2, c % 2
        shards.append(np.ascontiguousarray(x[b, r * TL:(r + 1) * TL, :].T))
    if FUSED:
        plan = [tuple(range(8))]
    else:
        plan = [(i,) for i in range(8)]
    for group in plan:
        pb, nc = _get_prog(group, list(group))
        names = set(pb.inputs.keys())
        maps = []
        for c in range(8):
            m = {k: v for k, v in mapr[c % 2].items() if k in names}
            m["xT"] = shards[c]
            maps.append(m)
        res = run_bass_kernel_spmd(nc, maps, core_ids=list(range(8)))
        shards = [np.ascontiguousarray(res.results[c]["yT"]) for c in range(8)]
    y = np.empty((4, 4096, 1024), np.float32)
    for c in range(8):
        b, r = c // 2, c % 2
        y[b, r * TL:(r + 1) * TL, :] = shards[c].T
    return y
